# version 19
# baseline (speedup 1.0000x reference)
import numpy as np
import ml_dtypes
import concourse.bass as bass
import concourse.mybir as mybir
from concourse.bass_utils import run_bass_kernel_spmd
from contextlib import ExitStack
from concourse import library_config

F32 = mybir.dt.float32
BF16 = mybir.dt.bfloat16
I8 = mybir.dt.int8
ALU = mybir.AluOpType
AF = mybir.ActivationFunctionType
AX = mybir.AxisListType.X
BF = ml_dtypes.bfloat16

H = 16
D = 64
E = 1024
TOK = 512
NCORES = 8
SEQ = 2048
SLOPES = [2.0 ** (-(h + 1) * 8.0 / H) for h in range(H)]
STEEP = [0, 1, 2, 3, 4]
EASY = list(range(5, 16))
NEG = -1.0e9
MAGIC = 12582912.0
SCALE = 0.125

ENGS = ("sync", "tensor", "vector", "scalar", "gpsimd")


def _bf_split(x):
    xf = np.asarray(x, np.float32)
    hi = xf.astype(BF)
    lo = (xf - hi.astype(np.float32)).astype(BF)
    return hi, lo


GROUP_G = [64, 128, 128, 256, 256] + [512] * 11


def _trims(c):
    if c <= 3:
        return 0, 128 * (c + 1)
    return 128 * c - 512, 512


def _ebias_layout():
    idx = {}
    n = 0
    for h in range(16):
        G = GROUP_G[h]
        for c in range(8):
            qs, qe = _trims(c)
            q0 = (qs // G) * G
            for qi0 in range(q0, qe, G):
                idx[(h, c, qi0, False)] = n
                n += 1
        for c in range(4, 8):
            idx[(h, c, 0, True)] = n
            n += 1
    return idx, n


EB_IDX, EB_N = _ebias_layout()


def _host_constants():
    c = {}
    c["ident"] = np.eye(128, dtype=BF)
    p = np.arange(128)
    eb = np.zeros((128, EB_N), np.float32)
    for (h, ch, qi0, blk0), col in EB_IDX.items():
        kw = 128 * ch + p
        if blk0:
            eb[:, col] = SLOPES[h] * (512.0 - kw)
        else:
            eb[:, col] = SLOPES[h] * (qi0 + GROUP_G[h] / 2.0 - kw)
    c["ebias"] = eb
    jj = np.arange(128)[None, :]
    pp = np.arange(128)[:, None]
    tri_post = np.zeros((128, 256), BF)
    tri_post[:, 0:128] = np.where(jj < pp, 0.0, 1.0).astype(BF)
    tri_post[:, 128:256] = np.where(jj > pp, 0.0, 1.0).astype(BF)
    c["tri_post"] = tri_post
    c["ones_bf"] = np.ones((1, 64), BF)
    o2 = np.zeros((2, 128), BF)
    o2[0, 0:64] = BF(1.0)
    o2[1, 64:128] = BF(1.0)
    c["ones2"] = o2
    return c


class Prog:
    def __init__(self, sem_names):
        self.ops = {e: [] for e in ENGS}
        self.cnt = {s: 0 for s in sem_names}

    def op(self, eng, fn, waits=(), inc=None, amt=1):
        w = [(s, v) for (s, v) in waits if v > 0]
        self.ops[eng].append((fn, w, inc, amt))
        if inc is not None:
            self.cnt[inc] += amt
            return self.cnt[inc]
        return None

    def drain(self, eng):
        self.ops[eng].append((lambda e: e.drain(), [], None, 1))

    def emit(self, block, sems):
        for eng_name in ENGS:
            ops_list = self.ops[eng_name]
            if not ops_list:
                continue
            deco = getattr(block, eng_name)

            def body(eng, ops_list=ops_list):
                hw = {}
                for fn, waits, inc, amt in ops_list:
                    for (s, v) in waits:
                        if v > hw.get(s, 0):
                            eng.wait_ge(sems[s], v)
                            hw[s] = v
                    inst = fn(eng)
                    if inc is not None:
                        inst.then_inc(sems[inc], amt)

            deco(body)


class Banks:
    def __init__(self):
        self.free_gate = [[] for _ in range(8)]
        self.order = list(range(8))

    def acquire(self):
        b = self.order.pop(0)
        return b, self.free_gate[b]

    def release(self, b, waits):
        self.free_gate[b] = list(waits)
        self.order.append(b)


def build_nc(n_seq=4, n_blk=4, taps=None):
    taps = taps or {}
    nc = bass.Bass("TRN2", target_bir_lowering=False)
    NT = n_seq * n_blk

    xh = nc.dram_tensor("x", [n_seq * n_blk * TOK, E], F32, kind="ExternalInput").ap()
    outh = nc.dram_tensor("out", [n_seq * n_blk * TOK, E], F32, kind="ExternalOutput").ap()
    w2h = nc.dram_tensor("w2", [16, 128, 8, 128], F32, kind="ExternalInput").ap()
    wvh = nc.dram_tensor("wv", [128, 8, 1024], F32, kind="ExternalInput").ap()
    wouth = nc.dram_tensor("wout", [128, 8, 1024], F32, kind="ExternalInput").ap()
    wqh = nc.dram_tensor("wq", [128, 8, 1024], I8, kind="ExternalInput").ap()
    wsbh = nc.dram_tensor("wsb", [128, 1024], F32, kind="ExternalInput").ap()
    biasbh = nc.dram_tensor("biasb", [128, 1024], BF16, kind="ExternalInput").ap()
    identh = nc.dram_tensor("ident", [128, 128], BF16, kind="ExternalInput").ap()
    ebiash = nc.dram_tensor("ebias", [128, EB_N], F32, kind="ExternalInput").ap()
    tripoh = nc.dram_tensor("tri_post", [128, 256], BF16, kind="ExternalInput").ap()
    onesh = nc.dram_tensor("ones_bf", [1, 64], BF16, kind="ExternalInput").ap()
    ones2h = nc.dram_tensor("ones2", [2, 128], BF16, kind="ExternalInput").ap()
    w2bf = nc.dram_tensor("w2bf", [16, 128, 8, 128], BF16, kind="Internal").ap()

    tap_out = {}
    for name, shape, dt in taps.get("list", []):
        tap_out[name] = nc.dram_tensor("tap_" + name, shape, dt, kind="ExternalOutput").ap()

    with ExitStack() as ctx:
        sb = lambda name, shape, dt: ctx.enter_context(nc.sbuf_tensor(name, shape, dt))
        wv_sb = sb("wv_sb", [128, 8 * 1024], BF16)
        wout_sb = sb("wout_sb", [128, 8 * 1024], BF16)
        wq_sb = sb("wq_sb", [128, 8 * 1024], BF16)
        wqk_buf = [sb(f"wqk{i}", [128, 2048], BF16) for i in range(2)]
        wsb_sb = sb("wsb_sb", [128, 1024], F32)
        biasb_sb = sb("biasb_sb", [128, 1024], BF16)
        ident_sb = sb("ident_sb", [128, 128], BF16)
        ebias_sb = sb("ebias_sb", [128, EB_N], F32)
        tripo_sb = sb("tripo_sb", [128, 256], BF16)
        ones_sb = sb("ones_sb", [1, 64], BF16)
        ones2_sb = sb("ones2_sb", [2, 128], BF16)

        x_tm = [sb(f"x_tm{i}", [128, 4096], F32) for i in range(2)]
        xn4 = sb("xn4", [128, 4096], BF16)
        xq2 = [sb(f"xq2_{i}", [128, 1024], BF16) for i in range(2)]
        xn_fm = sb("xn_fm", [128, 4096], BF16)
        xq_fm = [sb(f"xq_fm{i}", [128, 4096], BF16) for i in range(2)]
        q_fm = sb("q_fm", [128, 4096], BF16)
        k_st = sb("k_st", [128, 8192], BF16)
        v_st = sb("v_st", [128, 8320], BF16)
        o_fm = sb("o_fm", [128, 4096], BF16)
        o_stage = sb("o_stage", [64, 512], BF16)
        pe_buf = sb("pe_buf", [128, 8 * 512], BF16)
        zr_sb = sb("zr_sb", [1, 1024], BF16)
        zln2 = sb("zln2", [1, 1024], F32)
        zb_sb = sb("zb_sb", [128, 512], BF16)
        ffn1 = [sb(f"ffn1_{i}", [128, 512], F32) for i in range(2)]
        out_hf = [sb(f"outhf{i}", [128, 512], F32) for i in range(2)]
        stt = sb("stt", [128, 64], F32)
        stbn = sb("stbn", [128, 48], F32)
        xs_sb = sb("xs_sb", [128, 8], F32)

        C_MV, C_XMAX, C_XMIN, C_VEPS, C_SQ, C_R0, C_RTMP, C_RSTD = 0, 8, 12, 16, 20, 24, 28, 32
        C_BIASA, C_MX, C_AMAX, C_AM2, C_INV, C_ALPHA, C_BETA = 36, 40, 44, 48, 52, 56, 60


        psum = [ctx.enter_context(nc.psum_tensor(f"ps{i}", [128, 512], F32)) for i in range(8)]
        sem_names = ["X0", "X1", "WA", "WB", "SW", "LG", "SH0", "SH1", "ST0", "ST1", "TP", "T", "V", "S", "G"]
        sems = {s: ctx.enter_context(nc.semaphore(s)) for s in sem_names}
        P = Prog(sem_names)
        banks = Banks()

        def dma(engq, dst, src, inc, amt=16, waits=()):
            return P.op(engq, lambda e, d=dst, s=src: e.dma_start(out=d, in_=s),
                        waits=waits, inc=inc, amt=amt)

        ld0 = 0
        for dst, src in [(wsb_sb[:, :], wsbh), (biasb_sb[:, :], biasbh), (ident_sb[:, :], identh),
                         (ebias_sb[:, :], ebiash), (tripo_sb[:, :], tripoh), (ones_sb[:, :], onesh),
                         (ones2_sb[:, :], ones2h)]:
            ld0 = dma("sync", dst, src, "SW", waits=[("SW", ld0)])
        lg0 = dma("gpsimd", wv_sb[:, :], wvh.rearrange("p a b -> p (a b)"), "LG")
        lg0 = dma("gpsimd", wout_sb[:, :], wouth.rearrange("p a b -> p (a b)"), "LG",
                  waits=[("LG", lg0)])
        lg0 = dma("gpsimd", wq_sb[:, :], wqh.rearrange("p a b -> p (a b)"), "LG",
                  waits=[("LG", lg0)])
        w2f = w2h.rearrange("a p b c -> a p (b c)")
        w2b = w2bf.rearrange("a p b c -> a p (b c)")
        lgc, stc = P.cnt["LG"], P.cnt["SW"]
        for i in range(2):
            k3 = k_st[:, :].rearrange("p (a b) -> p a b", a=8)
            lgc = dma("gpsimd", k3, w2f[8 * i:8 * (i + 1)].rearrange("a p b -> p a b"),
                      "LG", waits=[("SW", stc), ("LG", lgc)])
            stc = dma("sync", w2b[8 * i:8 * (i + 1)].rearrange("a p b -> p a b"), k3,
                      "SW", waits=[("LG", lgc), ("SW", stc)])
        startup_ST, startup_LD, startup_LG = stc, ld0, lgc

        vones = v_st[:, :].rearrange("p (s h d) -> p s h d", s=8, h=16)[:, :, :, 64:65]
        g_ones = P.op("gpsimd", lambda e: e.memset(vones, 1.0), inc="G")

        x_free = [("X0", 0), ("X1", 0)]
        xn4_free = []
        xq2_free = [[], []]
        wqk_free = [[], []]
        ffn1_free = [[], []]
        outhf_free = [[], []]
        pe_free = [[] for _ in range(8)]
        ostg_free = [[]]
        zr_free = [[], []]
        zb_free = [[], []]
        zln_free = [[], []]
        prevT = 0
        TS = {}

        def tapit(name, src_ap, waits, engq="sync"):
            if name in tap_out:
                dma(engq, tap_out[name], src_ap, "TP",
                    waits=list(waits) + [("TP", P.cnt["TP"])])

        def tgeom(t):
            s_i, n_i = divmod(t, n_blk)
            return s_i, n_i, t % 2, n_i % 2, (n_i - 1) % 2, (s_i * n_blk + n_i) * TOK

        def emit_pre(t):
            s_i, n_i, par, slot, pslot, rowbase = tgeom(t)
            XS = "X%d" % par
            ld_x = dma("sync", x_tm[par][:, :].rearrange("p (a e) -> p a e", a=4),
                       xh[rowbase:rowbase + 512, :].rearrange("(a p) e -> p a e", p=128),
                       XS, waits=[x_free[par]])
            P.drain("vector")
            for tc in range(4):
                xs_ap = x_tm[par][:, 1024 * tc:1024 * (tc + 1)]
                P.op("vector", lambda e, a=xs_ap, tc=tc: e.bn_stats(
                    out=stbn[:, 12 * tc:12 * tc + 6], in_=a[:, 0:512]),
                    waits=[(XS, ld_x)])
                P.op("vector", lambda e, a=xs_ap, tc=tc: e.bn_stats(
                    out=stbn[:, 12 * tc + 6:12 * tc + 12], in_=a[:, 512:1024]))
            P.drain("vector")
            for tc in range(4):
                P.op("vector", lambda e, tc=tc: e.bn_aggr(
                    out=stt[:, C_MV + 2 * tc:C_MV + 2 * tc + 2], in_=stbn[:, 12 * tc:12 * (tc + 1)]))
                P.op("vector", lambda e, tc=tc, a=x_tm[par]: e.reduce_max(
                    out=stt[:, C_XMAX + tc:C_XMAX + tc + 1], in_=a[:, 1024 * tc:1024 * (tc + 1)], axis=AX))
                P.op("vector", lambda e, tc=tc, a=x_tm[par]: e.tensor_reduce(
                    out=stt[:, C_XMIN + tc:C_XMIN + tc + 1], in_=a[:, 1024 * tc:1024 * (tc + 1)],
                    axis=AX, op=ALU.min))
            P.drain("vector")
            mv3 = stt[:, :].rearrange("p (a b) -> p a b", b=2)
            mv_var = mv3[:, 0:4, 1:2]
            mv_mean = mv3[:, 0:4, 0:1]
            P.op("vector", lambda e, mv_var=mv_var: e.tensor_single_scalar(
                out=stt[:, C_VEPS:C_VEPS + 4], in_=mv_var, scalar=1e-5, op=ALU.add))
            P.drain("vector")
            seed_i = stt[:, C_R0:C_R0 + 4].bitcast(mybir.dt.int32)
            P.op("vector", lambda e, seed_i=seed_i: e.tensor_single_scalar(
                out=seed_i, in_=stt[:, C_VEPS:C_VEPS + 4].bitcast(mybir.dt.int32),
                scalar=1, op=ALU.arith_shift_right))
            P.drain("vector")
            P.op("vector", lambda e, seed_i=seed_i: e.tensor_scalar(
                out=seed_i, in0=seed_i, scalar1=-1, scalar2=0x5f3759df,
                op0=ALU.mult, op1=ALU.add))
            P.drain("vector")
            for _nit in range(3):
                last = (_nit == 2)
                P.op("vector", lambda e: e.tensor_tensor(out=stt[:, C_RTMP:C_RTMP + 4],
                                                         in0=stt[:, C_R0:C_R0 + 4],
                                                         in1=stt[:, C_R0:C_R0 + 4], op=ALU.mult))
                P.drain("vector")
                P.op("vector", lambda e: e.tensor_tensor(out=stt[:, C_RTMP:C_RTMP + 4],
                                                         in0=stt[:, C_RTMP:C_RTMP + 4],
                                                         in1=stt[:, C_VEPS:C_VEPS + 4], op=ALU.mult))
                P.drain("vector")
                P.op("vector", lambda e: e.tensor_scalar(out=stt[:, C_RTMP:C_RTMP + 4],
                                                         in0=stt[:, C_RTMP:C_RTMP + 4],
                                                         scalar1=-0.5, scalar2=1.5,
                                                         op0=ALU.mult, op1=ALU.add))
                P.drain("vector")
                dst_col = C_RSTD if last else C_R0
                P.op("vector", lambda e, dst_col=dst_col: e.tensor_tensor(
                    out=stt[:, dst_col:dst_col + 4], in0=stt[:, C_RTMP:C_RTMP + 4],
                    in1=stt[:, C_R0:C_R0 + 4], op=ALU.mult))
                P.drain("vector")
            P.op("vector", lambda e, mv_mean=mv_mean: e.scalar_tensor_tensor(
                out=stt[:, C_BIASA:C_BIASA + 4], in0=mv_mean, scalar=-1.0,
                in1=stt[:, C_RSTD:C_RSTD + 4], op0=ALU.mult, op1=ALU.mult))
            P.op("vector", lambda e, mv_mean=mv_mean: e.tensor_tensor(
                out=stt[:, C_MX:C_MX + 4], in0=stt[:, C_XMAX:C_XMAX + 4],
                in1=mv_mean, op=ALU.subtract))
            P.op("vector", lambda e, mv_mean=mv_mean: e.tensor_tensor(
                out=stt[:, C_RTMP:C_RTMP + 4], in0=mv_mean,
                in1=stt[:, C_XMIN:C_XMIN + 4], op=ALU.subtract))
            P.drain("vector")
            P.op("vector", lambda e: e.tensor_tensor(out=stt[:, C_MX:C_MX + 4],
                                                     in0=stt[:, C_MX:C_MX + 4],
                                                     in1=stt[:, C_RTMP:C_RTMP + 4], op=ALU.max))
            P.drain("vector")
            P.op("vector", lambda e: e.tensor_tensor(out=stt[:, C_AMAX:C_AMAX + 4],
                                                     in0=stt[:, C_MX:C_MX + 4],
                                                     in1=stt[:, C_RSTD:C_RSTD + 4], op=ALU.mult))
            P.drain("vector")
            P.op("vector", lambda e: e.tensor_single_scalar(
                out=stt[:, C_AM2:C_AM2 + 4], in_=stt[:, C_AMAX:C_AMAX + 4], scalar=1e-6, op=ALU.max))
            P.drain("vector")
            P.op("vector", lambda e: e.reciprocal(out=stt[:, C_INV:C_INV + 4], in_=stt[:, C_AM2:C_AM2 + 4]))
            P.op("vector", lambda e, par=par: e.tensor_single_scalar(
                out=xs_sb[:, 4 * par:4 * par + 4], in_=stt[:, C_AM2:C_AM2 + 4],
                scalar=1.0 / 127.0, op=ALU.mult))
            P.drain("vector")
            P.op("vector", lambda e: e.tensor_tensor(out=stt[:, C_RTMP:C_RTMP + 4],
                                                     in0=stt[:, C_INV:C_INV + 4],
                                                     in1=stt[:, C_RSTD:C_RSTD + 4], op=ALU.mult))
            P.drain("vector")
            P.op("vector", lambda e: e.tensor_single_scalar(
                out=stt[:, C_ALPHA:C_ALPHA + 4], in_=stt[:, C_RTMP:C_RTMP + 4], scalar=127.0, op=ALU.mult))
            P.drain("vector")
            P.op("vector", lambda e, mv_mean=mv_mean: e.scalar_tensor_tensor(
                out=stt[:, C_BETA:C_BETA + 4], in0=mv_mean, scalar=-1.0,
                in1=stt[:, C_ALPHA:C_ALPHA + 4], op0=ALU.mult, op1=ALU.mult))
            v_stats = P.op("vector", lambda e: e.engine_nop(), inc="V")
            toks = []
            for tc in range(4):
                xs_ap = x_tm[par][:, 1024 * tc:1024 * (tc + 1)]
                v_xn = P.op("vector", lambda e, tc=tc, a=xs_ap: e.tensor_scalar(
                    out=xn4[:, 1024 * tc:1024 * (tc + 1)], in0=a,
                    scalar1=stt[:, C_RSTD + tc:C_RSTD + tc + 1],
                    scalar2=stt[:, C_BIASA + tc:C_BIASA + tc + 1],
                    op0=ALU.mult, op1=ALU.add),
                    waits=[("V", v_stats)] + xn4_free, inc="V")
                toks.append(("V", v_xn))
            TS[t] = {"ldx": ld_x, "XS": XS, "vstats": v_stats, "xnqtok": toks}
            tapit("stats", stt[:, :], [("V", v_stats)])

        def emit_xnq_transpose(t):
            nonlocal xn4_free
            s_i, n_i, par, slot, pslot, rowbase = tgeom(t)
            v_stats = TS[t]["vstats"]
            toks = TS[t]["xnqtok"]
            vq_toks = {}

            def xq_chain(tc):
                tcp = tc % 2
                xs_ap = x_tm[par][:, 1024 * tc:1024 * (tc + 1)]
                qb, qbw = banks.acquire()
                qb2, qbw2 = banks.acquire()
                P.op("vector", lambda e, tc=tc, a=xs_ap, qb=qb: e.tensor_scalar(
                    out=psum[qb][:, :], in0=a[:, 0:512], scalar1=stt[:, C_ALPHA + tc:C_ALPHA + tc + 1],
                    scalar2=stt[:, C_BETA + tc:C_BETA + tc + 1], op0=ALU.mult, op1=ALU.add),
                    waits=[("V", v_stats)] + qbw, inc="V")
                P.op("vector", lambda e, tc=tc, a=xs_ap, qb2=qb2: e.tensor_scalar(
                    out=psum[qb2][:, :], in0=a[:, 512:1024], scalar1=stt[:, C_ALPHA + tc:C_ALPHA + tc + 1],
                    scalar2=stt[:, C_BETA + tc:C_BETA + tc + 1], op0=ALU.mult, op1=ALU.add),
                    waits=qbw2, inc="V")
                P.drain("vector")
                v_q2 = P.op("vector", lambda e, o=xq2[tcp], qb=qb: e.tensor_scalar(
                    out=o[:, 0:512], in0=psum[qb][:, :], scalar1=MAGIC, scalar2=MAGIC,
                    op0=ALU.add, op1=ALU.subtract), waits=xq2_free[tcp], inc="V")
                banks.release(qb, [("V", v_q2)])
                v_q2 = P.op("vector", lambda e, o=xq2[tcp], qb2=qb2: e.tensor_scalar(
                    out=o[:, 512:1024], in0=psum[qb2][:, :], scalar1=MAGIC, scalar2=MAGIC,
                    op0=ALU.add, op1=ALU.subtract), inc="V")
                banks.release(qb2, [("V", v_q2)])
                vq_toks[tc] = v_q2

            def do_transpose(tc, half):
                tcp = tc % 2
                if half == 0:
                    wgate = [toks[tc]]
                    lsrc, loff, dst_fm = xn4, 1024 * tc, xn_fm
                else:
                    wgate = [("V", vq_toks[tc])]
                    lsrc, loff, dst_fm = xq2[tcp], 0, xq_fm[par]
                for g in range(2):
                    b, bw = banks.acquire()
                    for j in range(4):
                        ec = g * 4 + j
                        tk = P.op("tensor", lambda e, b=b, j=j, ec=ec, lsrc=lsrc, loff=loff: e.matmul(
                            out=psum[b][:, 128 * j:128 * (j + 1)],
                            lhsT=lsrc[:, loff + 128 * ec:loff + 128 * (ec + 1)],
                            rhs=ident_sb[:, :], start=True, stop=True),
                            waits=(wgate + bw + [("SW", startup_LD), ("T", prevT)]) if j == 0 else (),
                            inc="T")
                    dst = dst_fm[:, :].rearrange("p (a b) -> p a b", b=512)[
                        :, g * 4:g * 4 + 4, 128 * tc:128 * (tc + 1)]
                    se = P.op("scalar", lambda e, b=b, d=dst: e.copy(
                        out=d, in_=psum[b][:, :].rearrange("p (a b) -> p a b", a=4)),
                        waits=[("T", tk), ("T", prevT)], inc="S")
                    banks.release(b, [("S", se)])
                if half == 1:
                    xq2_free[tcp] = [("T", P.cnt["T"])]

            xq_chain(0)
            xq_chain(1)
            for tc in range(4):
                do_transpose(tc, 0)
            do_transpose(0, 1)
            xq_chain(2)
            do_transpose(1, 1)
            xq_chain(3)
            do_transpose(2, 1)
            do_transpose(3, 1)
            xn4_free = [("T", P.cnt["T"])]
            TS[t]["sfm"] = ("S", P.cnt["S"])
            tapit("xn_fm", xn_fm[:, :], [TS[t]["sfm"]])
            tapit("xq_fm", xq_fm[par][:, :], [TS[t]["sfm"]])

        def emit_qkv(t):
            s_i, n_i, par, slot, pslot, rowbase = tgeom(t)
            qk_order = list(range(16))
            qk_tok = {}
            fm_gate = TS[t]["sfm"]
            for fgrp in range(8):
                fp = fgrp % 2
                WS = "WA" if fp == 0 else "WB"
                lw = dma("sync", wqk_buf[fp][:, :],
                         w2bf[2 * fgrp:2 * (fgrp + 1)].rearrange("a p b c -> p a (b c)"), WS,
                         waits=wqk_free[fp] + ([("SW", startup_ST)] if (t == 0 and fgrp < 2) else []))
                tk = None
                for fi in range(2):
                    fc = 2 * fgrp + fi
                    b, bw = banks.acquire()
                    for ec in range(8):
                        tk = P.op("tensor", lambda e, b=b, ec=ec, fp=fp, fi=fi: e.matmul(
                            out=psum[b][:, :],
                            lhsT=wqk_buf[fp][:, 1024 * fi + 128 * ec:1024 * fi + 128 * (ec + 1)],
                            rhs=xn_fm[:, 512 * ec:512 * (ec + 1)], start=(ec == 0), stop=(ec == 7)),
                            waits=([(WS, lw), fm_gate] + bw) if ec == 0 else (),
                            inc=("T" if ec == 7 else None))
                    wqk_free[fp] = [("T", tk)]
                    if fc < 8:
                        dst = q_fm[:, 512 * fc:512 * (fc + 1)]
                    else:
                        pr = fc - 8
                        dst = k_st[:, (pr * 2 + slot) * 512:(pr * 2 + slot) * 512 + 512]
                    se = P.op("scalar", lambda e, b=b, d=dst: e.copy(out=d, in_=psum[b][:, :]),
                              waits=[("T", tk), ("T", prevT)] + ([("SW", startup_ST)] if t == 0 else []),
                              inc="S")
                    banks.release(b, [("S", se)])
                    qk_tok[fc % 8] = se
            TS[t]["qw"] = dict(qk_tok)
            v_tok = {}
            for half in range(2):
                for tc in range(4):
                    b, bw = banks.acquire()
                    tk = None
                    for ec in range(8):
                        tk = P.op("tensor", lambda e, b=b, ec=ec, tc=tc, half=half: e.matmul(
                            out=psum[b][:, :],
                            lhsT=xn_fm[:, 512 * ec + 128 * tc:512 * ec + 128 * (tc + 1)],
                            rhs=wv_sb[:, 1024 * ec + 512 * half:1024 * ec + 512 * (half + 1)],
                            start=(ec == 0), stop=(ec == 7)),
                            waits=([fm_gate, ("LG", startup_LG)] + bw) if ec == 0 else (),
                            inc=("T" if ec == 7 else None))
                    dst = v_st[:, :].rearrange("p (s h d) -> p s h d", s=8, h=16)[
                        :, slot * 4 + tc, half * 8:half * 8 + 8, 0:64]
                    se = P.op("scalar", lambda e, b=b, d=dst: e.copy(
                        out=d, in_=psum[b][:, :].rearrange("p (h d) -> p h d", h=8)),
                        waits=[("T", tk), ("G", g_ones), ("T", prevT)], inc="S")
                    banks.release(b, [("S", se)])
                    v_tok[half] = se
            TS[t]["kv"] = dict(v_tok)
            last_tok = P.cnt["S"]
            tapit("q_fm", q_fm[:, :], [("S", last_tok)])
            tapit("k_st", k_st[:, :], [("S", last_tok)])
            tapit("v_st", v_st[:, :], [("S", last_tok)])

        def emit_attn(t, hooks=None):
            nonlocal zr_free, zb_free, zln_free
            hooks = hooks or {}
            s_i, n_i, par, slot, pslot, rowbase = tgeom(t)
            qk_tok = TS[t]["qw"]
            v_tok = TS[t]["kv"]
            chunks = list(range(8)) if n_i > 0 else list(range(4, 8))
            ofm_by_pair = {pr: [] for pr in range(8)}
            divq = []
            divq2 = []

            def k_slice(hh, kw0, w):
                pr, hp = hh // 2, hh % 2
                sl = pslot if kw0 < 512 else slot
                tok0 = kw0 if kw0 < 512 else kw0 - 512
                return k_st[64 * hp:64 * (hp + 1),
                            (pr * 2 + sl) * 512 + tok0:(pr * 2 + sl) * 512 + tok0 + w]

            def v_slice(hh, kw0, ncol):
                sl = pslot if kw0 < 512 else slot
                tok0 = kw0 if kw0 < 512 else kw0 - 512
                tckv = tok0 // 128
                off = ((sl * 4 + tckv) * 16 + hh) * 65
                return v_st[:, off:off + ncol]

            def flush_div_a(k=2):
                nonlocal zr_free, zln_free
                jobs = [divq.pop(0) for _ in range(min(k, len(divq)))]
                for (hh, ob, tkp) in jobs:
                    sl = hh % 2
                    P.op("scalar", lambda e, ob=ob, sl=sl: e.activation(
                        out=zln2[0:1, 512 * sl:512 * (sl + 1)], in_=psum[ob][64:65, :], func=AF.Ln),
                        waits=[("T", tkp)] + zln_free[sl], inc="S")
                    P.drain("scalar")
                    vz = P.op("scalar", lambda e, sl=sl: e.activation(
                        out=zr_sb[0:1, 512 * sl:512 * (sl + 1)],
                        in_=zln2[0:1, 512 * sl:512 * (sl + 1)], func=AF.Exp, scale=-1.0),
                        waits=zr_free[sl], inc="S")
                    zln_free[sl] = [("S", vz)]
                    divq2.append((hh, ob, vz))

            def flush_div_b(k=2):
                nonlocal zr_free, zb_free
                jobs = [divq2.pop(0) for _ in range(min(k, len(divq2)))]
                if not jobs:
                    return
                assert len(jobs) == 2 and jobs[0][0] % 2 == 0, [j[0] for j in jobs]
                (hA, obA, vzA), (hB, obB, vzB) = jobs
                pr = hA // 2
                zbk, zbw = banks.acquire()
                P.op("tensor", lambda e, zbk=zbk: e.matmul(
                    out=psum[zbk][0:64, :], lhsT=ones_sb[0:1, :], rhs=zr_sb[0:1, 0:512],
                    start=True, stop=True),
                    waits=[("S", vzA)] + zbw, inc="T")
                tzb = P.op("tensor", lambda e, zbk=zbk: e.matmul(
                    out=psum[zbk][64:128, :], lhsT=ones_sb[0:1, :], rhs=zr_sb[0:1, 512:1024],
                    start=True, stop=True, tile_position=(0, 64)),
                    waits=[("S", vzB)], inc="T")
                zr_free[0] = [("T", tzb)]
                zr_free[1] = [("T", tzb)]
                szb = P.op("vector", lambda e, zbk=zbk: e.tensor_copy(
                    out=zb_sb[:, :], in_=psum[zbk][:, :]),
                    waits=[("T", tzb)] + zb_free[0], inc="V")
                banks.release(zbk, [("V", szb)])
                dsto = o_fm[0:64, pr * 512:pr * 512 + 512]
                vdo = P.op("vector", lambda e, obA=obA, d=dsto: e.tensor_tensor(
                    out=d, in0=psum[obA][0:64, :], in1=zb_sb[0:64, :], op=ALU.mult),
                    waits=[("V", szb), ("T", prevT)], inc="V")
                banks.release(obA, [("V", vdo)])
                ofm_by_pair[pr].append(("V", vdo))
                vdo2 = P.op("vector", lambda e, obB=obB: e.tensor_tensor(
                    out=o_stage[0:64, :], in0=psum[obB][0:64, :],
                    in1=zb_sb[64:128, :], op=ALU.mult),
                    waits=[("V", szb)] + ostg_free[0], inc="V")
                banks.release(obB, [("V", vdo2)])
                zb_free[0] = [("V", vdo2)]
                sh = dma("gpsimd", o_fm[64:128, pr * 512:pr * 512 + 512],
                         o_stage[0:64, :], "SH0",
                         waits=[("V", vdo2), ("T", prevT)])
                ostg_free[0] = [("SH0", sh)]
                ofm_by_pair[pr].append(("SH0", sh))

            for pr_grp in range(8):
                grp = [2 * pr_grp, 2 * pr_grp + 1]
                obk = {hh: banks.acquire() for hh in grp}
                pv_first = {hh: True for hh in grp}
                tk_last = {}
                pe_slot_i = 0
                pending = []

                def emit_pv(it):
                    (hh, cch, qsp, wp, slp, gmp, sexpp, lastp) = it
                    kw0p = 128 * cch
                    ob, obw = obk[hh]
                    vsl = v_slice(hh, kw0p, 65)
                    if n_i > 0:
                        fresh0, fresh1 = (128 * cch, 128 * (cch + 1)) if cch <= 3 else (None, None)
                    else:
                        fresh0, fresh1 = (0, 512) if cch == 4 else (None, None)
                    segs = []
                    if fresh0 is None:
                        segs.append((qsp, wp))
                    else:
                        if fresh0 > qsp:
                            segs.append((qsp, fresh0 - qsp))
                        segs.append((fresh0, fresh1 - fresh0))
                    tkpv = None
                    for (c0, cw) in segs:
                        tkpv = P.op("tensor", lambda e, ob=ob, c0=c0, cw=cw, qsp=qsp, vsl=vsl,
                                    slp=slp, first=pv_first[hh], lastp=lastp: e.matmul(
                            out=psum[ob][0:65, c0:c0 + cw], lhsT=vsl,
                            rhs=pe_buf[:, 512 * slp + (c0 - qsp):512 * slp + (c0 - qsp) + cw],
                            start=first, stop=lastp),
                            waits=[("V", gmp), ("S", sexpp), ("S", v_tok[hh // 8])]
                                  + (obw if pv_first[hh] else []), inc="T")
                        pv_first[hh] = False
                        obk[hh] = (ob, [])
                    pe_free[slp] = [("T", tkpv)]
                    tk_last[hh] = tkpv

                for ci, c in enumerate(chunks):
                    qs, qe = _trims(c)
                    w = qe - qs
                    kw0 = 128 * c
                    cur_items = []
                    for hh in grp:
                        hp2 = hh % 2
                        base2 = 64 * hp2
                        pr2 = hh // 2
                        b, bw = banks.acquire()
                        ksl = k_slice(hh, kw0, 128)
                        tk = P.op("tensor", lambda e, b=b, base2=base2, pr2=pr2,
                                  qs=qs, w=w, ksl=ksl: e.matmul(
                            out=psum[b][:, 0:w], lhsT=ksl,
                            rhs=q_fm[base2:base2 + 64, pr2 * 512 + qs:pr2 * 512 + qs + w],
                            start=True, stop=True, tile_position=(base2, 0)),
                            waits=bw + [("S", qk_tok[pr2])], inc="T")
                        sl_pe = pe_slot_i % 8
                        pe_slot_i += 1
                        G = GROUP_G[hh]
                        sexp = 0
                        if n_i == 0:
                            gslices = [(qs, qe, EB_IDX[(hh, c, 0, True)])]
                        else:
                            q0 = (qs // G) * G
                            gslices = [(max(qs, qi0), min(qe, qi0 + G), EB_IDX[(hh, c, qi0, False)])
                                       for qi0 in range(q0, qe, G)]
                        if c <= 3:
                            gslices = gslices[::-1]
                        first_a0 = gslices[0][0]
                        tri_done = 0
                        for gi, (a0, a1, ecol) in enumerate(gslices):
                            sexp = P.op("scalar", lambda e, b=b, a0=a0, a1=a1, qs=qs,
                                        sl_pe=sl_pe, ecol=ecol: e.activation(
                                out=pe_buf[:, 512 * sl_pe + a0 - qs:512 * sl_pe + a1 - qs],
                                in_=psum[b][:, a0 - qs:a1 - qs],
                                func=AF.Exp, scale=SCALE, bias=ebias_sb[:, ecol:ecol + 1]),
                                waits=([("T", tk), ("SW", startup_LD)] + pe_free[sl_pe])
                                      if gi == 0 else (), inc="S")
                            tri_end = (128 * c - qs + 128) if c <= 3 else 128
                            tri_start = (128 * c - qs) if c <= 3 else 0
                            if a0 - qs < tri_end and a1 - qs > tri_start:
                                tri_done = sexp
                        banks.release(b, [("S", sexp)])
                        tloc = (128 * c - qs) if c <= 3 else 0
                        tsel = 128 if c <= 3 else 0
                        gm = P.op("vector", lambda e, sl_pe=sl_pe, tloc=tloc, tsel=tsel: e.tensor_tensor(
                            out=pe_buf[:, 512 * sl_pe + tloc:512 * sl_pe + tloc + 128],
                            in0=pe_buf[:, 512 * sl_pe + tloc:512 * sl_pe + tloc + 128],
                            in1=tripo_sb[:, tsel:tsel + 128], op=ALU.mult),
                            waits=[("S", tri_done)], inc="V")
                        cur_items.append((hh, c, qs, w, sl_pe, gm, sexp, ci == len(chunks) - 1))
                    pending.append(cur_items)
                    if len(pending) > 3:
                        for it in pending.pop(0):
                            emit_pv(it)
                    if ci == 1:
                        flush_div_a(2)
                    if ci == len(chunks) - 2:
                        flush_div_b(2)
                for items in pending:
                    for it in items:
                        emit_pv(it)
                for hh in grp:
                    divq.append((hh, obk[hh][0], tk_last[hh]))
                if pr_grp in hooks:
                    hooks[pr_grp]()
            flush_div_a(len(divq))
            flush_div_b(len(divq2))

            TS[t]["ofm_pair"] = ofm_by_pair
            allofm = [tok for pr in range(8) for tok in ofm_by_pair[pr]]
            tapit("o_fm", o_fm[:, :], allofm)

        def emit_proj(t):
            s_i, n_i, par, slot, pslot, rowbase = tgeom(t)
            ofm_pair = TS[t]["ofm_pair"]
            fm_gate = TS[t]["sfm"]
            for tc in range(4):
                att_b = []
                ffn_b = []
                for half in range(2):
                    b, bw = banks.acquire()
                    tk = None
                    for pr in range(8):
                        tk = P.op("tensor", lambda e, b=b, pr=pr, tc=tc, half=half: e.matmul(
                            out=psum[b][:, :],
                            lhsT=o_fm[:, pr * 512 + 128 * tc:pr * 512 + 128 * (tc + 1)],
                            rhs=wout_sb[:, 1024 * pr + 512 * half:1024 * pr + 512 * (half + 1)],
                            start=(pr == 0), stop=(pr == 7)),
                            waits=ofm_pair[pr] + ((bw + [("LG", startup_LG)]) if pr == 0 else []),
                            inc=("T" if pr == 7 else None))
                    att_b.append((b, tk))
                for half in range(2):
                    b, bw = banks.acquire()
                    tk = None
                    for ec in range(8):
                        tk = P.op("tensor", lambda e, b=b, ec=ec, tc=tc, half=half, par=par: e.matmul(
                            out=psum[b][:, :],
                            lhsT=xq_fm[par][:, 512 * ec + 128 * tc:512 * ec + 128 * (tc + 1)],
                            rhs=wq_sb[:, 1024 * ec + 512 * half:1024 * ec + 512 * (half + 1)],
                            start=(ec == 0), stop=(ec == 7)),
                            waits=(bw + [fm_gate]) if ec == 0 else (),
                            inc=("T" if ec == 7 else None))
                    ffn_b.append((b, tk))
                for half in range(2):
                    ab, atk = att_b[half]
                    fb, ftk = ffn_b[half]
                    u = ffn1[half]
                    v1 = P.op("vector", lambda e, fb=fb, u=u, tc=tc, half=half, par=par: e.scalar_tensor_tensor(
                        out=u[:, :], in0=psum[fb][:, :], scalar=xs_sb[:, 4 * par + tc:4 * par + tc + 1],
                        in1=wsb_sb[:, 512 * half:512 * (half + 1)], op0=ALU.mult, op1=ALU.mult),
                        waits=[("T", ftk)] + ffn1_free[half], inc="V")
                    banks.release(fb, [("V", v1)])
                    P.drain("vector")
                    P.op("vector", lambda e, u=u, half=half: e.tensor_tensor(
                        out=u[:, :], in0=u[:, :], in1=biasb_sb[:, 512 * half:512 * (half + 1)],
                        op=ALU.add))
                    P.drain("vector")
                    P.op("vector", lambda e, u=u, tc=tc, half=half, par=par: e.tensor_tensor(
                        out=u[:, :], in0=u[:, :],
                        in1=x_tm[par][:, 1024 * tc + 512 * half:1024 * tc + 512 * (half + 1)],
                        op=ALU.add))
                    P.drain("vector")
                    v4 = P.op("vector", lambda e, u=u, ab=ab, half=half: e.tensor_tensor(
                        out=out_hf[half][:, :], in0=psum[ab][:, :],
                        in1=u[:, :], op=ALU.add),
                        waits=[("T", atk)] + outhf_free[half], inc="V")
                    banks.release(ab, [("V", v4)])
                    ffn1_free[half] = [("V", v4)]
                    STS = "ST%d" % half
                    st_o = dma("gpsimd",
                               outh[rowbase + 128 * tc:rowbase + 128 * (tc + 1),
                                    512 * half:512 * (half + 1)],
                               out_hf[half][:, :], STS, waits=[("V", v4)])
                    outhf_free[half] = [(STS, st_o)]
            x_free[par] = ("V", P.cnt["V"])

        emit_pre(0)
        emit_xnq_transpose(0)
        emit_qkv(0)
        for t in range(NT):
            if t + 1 < NT:
                emit_pre(t + 1)
            emit_attn(t)
            if t + 1 < NT:
                emit_xnq_transpose(t + 1)
            emit_proj(t)
            prevT = P.cnt["T"]
            if t + 1 < NT:
                emit_qkv(t + 1)
            TS.pop(t - 1, None)

        P.op("sync", lambda e: e.nop(),
             waits=[(s, P.cnt[s]) for s in
                    ("ST0", "ST1", "SH0", "SH1", "X0", "X1", "WA", "WB", "SW", "LG", "TP")])

        block = ctx.enter_context(nc.Block())
        P.emit(block, sems)
    return nc


def _shard_inputs(L0_attn_x, L0_attn_w_in, L0_attn_w_out, L0_ffn_weight_q,
                  L0_ffn_weight_scale, L0_ffn_bias, n_seq=4, n_cores=NCORES):
    x = np.ascontiguousarray(np.asarray(L0_attn_x, np.float32))
    w_in = np.asarray(L0_attn_w_in, np.float32)
    w_out = np.asarray(L0_attn_w_out, np.float32)
    wq = np.asarray(L0_ffn_weight_q, np.int8)
    ws = np.asarray(L0_ffn_weight_scale, np.float32)
    bi = np.asarray(L0_ffn_bias, np.float32)

    w_inT = np.ascontiguousarray(w_in.T)
    w2 = np.ascontiguousarray(w_inT.reshape(8, 128, 24, 128).transpose(2, 1, 0, 3)[0:16])
    wv = np.ascontiguousarray(w_inT[:, 2048:3072].reshape(8, 128, 1024).transpose(1, 0, 2))
    wout = np.ascontiguousarray(w_out.T.reshape(8, 128, 1024).transpose(1, 0, 2))
    wqT = np.ascontiguousarray(wq.T.reshape(8, 128, 1024).transpose(1, 0, 2))
    wsb = np.ascontiguousarray(np.broadcast_to(ws[None, :], (128, 1024)))
    biasb = np.ascontiguousarray(np.broadcast_to(bi[None, :], (128, 1024)).astype(BF))
    consts = _host_constants()

    shared = dict(w2=w2, wv=wv, wout=wout, wq=wqT, wsb=wsb, biasb=biasb,
                  ident=consts["ident"], ebias=consts["ebias"],
                  tri_post=consts["tri_post"], ones_bf=consts["ones_bf"],
                  ones2=consts["ones2"])
    in_maps = []
    for c in range(n_cores):
        m = dict(shared)
        m["x"] = np.ascontiguousarray(x[c * n_seq:(c + 1) * n_seq].reshape(n_seq * SEQ, E))
        in_maps.append(m)
    return in_maps


def kernel(L0_attn_x, L0_attn_w_in, L0_attn_w_out, L0_ffn_weight_q,
           L0_ffn_weight_scale, L0_ffn_bias):
    in_maps = _shard_inputs(L0_attn_x, L0_attn_w_in, L0_attn_w_out,
                            L0_ffn_weight_q, L0_ffn_weight_scale, L0_ffn_bias)
    nc = build_nc(n_seq=4, n_blk=4)
    res = run_bass_kernel_spmd(nc, in_maps, core_ids=list(range(NCORES)))
    out = np.concatenate([r["out"].reshape(4, SEQ, E) for r in res.results], axis=0)
    return np.ascontiguousarray(out.astype(np.float32))


# version 20
# speedup vs baseline: 1.0578x; 1.0578x over previous
import numpy as np
import ml_dtypes
import concourse.bass as bass
import concourse.mybir as mybir
from concourse.bass_utils import run_bass_kernel_spmd
from contextlib import ExitStack
from concourse import library_config

F32 = mybir.dt.float32
BF16 = mybir.dt.bfloat16
I8 = mybir.dt.int8
ALU = mybir.AluOpType
AF = mybir.ActivationFunctionType
AX = mybir.AxisListType.X
BF = ml_dtypes.bfloat16

H = 16
D = 64
E = 1024
TOK = 512
NCORES = 8
SEQ = 2048
SLOPES = [2.0 ** (-(h + 1) * 8.0 / H) for h in range(H)]
STEEP = [0, 1, 2, 3, 4]
EASY = list(range(5, 16))
NEG = -1.0e9
MAGIC = 12582912.0
SCALE = 0.125

ENGS = ("sync", "tensor", "vector", "scalar", "gpsimd")


def _bf_split(x):
    xf = np.asarray(x, np.float32)
    hi = xf.astype(BF)
    lo = (xf - hi.astype(np.float32)).astype(BF)
    return hi, lo


GROUP_G = [64, 128, 128, 256, 256] + [512] * 11


def _trims(c):
    if c <= 3:
        return 0, 128 * (c + 1)
    return 128 * c - 512, 512


def _ebias_layout():
    idx = {}
    n = 0
    for h in range(16):
        G = GROUP_G[h]
        for c in range(8):
            qs, qe = _trims(c)
            q0 = (qs // G) * G
            for qi0 in range(q0, qe, G):
                idx[(h, c, qi0, False)] = n
                n += 1
        for c in range(4, 8):
            idx[(h, c, 0, True)] = n
            n += 1
    return idx, n


EB_IDX, EB_N = _ebias_layout()


def _chunk_groups(hh, c, n_i):
    qs, qe = _trims(c)
    G = GROUP_G[hh]
    if n_i == 0:
        return qs, [(qs, qe, EB_IDX[(hh, c, 0, True)])]
    q0 = (qs // G) * G
    out = []
    for qi0 in range(q0, qe, G):
        a0, a1 = max(qs, qi0), min(qe, qi0 + G)
        if SLOPES[hh] * (128 * c - qi0 - G / 2.0) > 105.0:
            continue
        out.append((a0, a1, EB_IDX[(hh, c, qi0, False)]))
    if not out:
        return None, []
    qsk = out[0][0]
    return qsk, out


def _host_constants():
    c = {}
    c["ident"] = np.eye(128, dtype=BF)
    p = np.arange(128)
    eb = np.zeros((128, EB_N), np.float32)
    for (h, ch, qi0, blk0), col in EB_IDX.items():
        kw = 128 * ch + p
        if blk0:
            eb[:, col] = SLOPES[h] * (512.0 - kw)
        else:
            eb[:, col] = SLOPES[h] * (qi0 + GROUP_G[h] / 2.0 - kw)
    c["ebias"] = eb
    jj = np.arange(128)[None, :]
    pp = np.arange(128)[:, None]
    tri_post = np.zeros((128, 256), BF)
    tri_post[:, 0:128] = np.where(jj < pp, 0.0, 1.0).astype(BF)
    tri_post[:, 128:256] = np.where(jj > pp, 0.0, 1.0).astype(BF)
    c["tri_post"] = tri_post
    c["ones_bf"] = np.ones((1, 64), BF)
    return c


class Prog:
    def __init__(self, sem_names):
        self.ops = {e: [] for e in ENGS}
        self.cnt = {s: 0 for s in sem_names}

    def op(self, eng, fn, waits=(), inc=None, amt=1):
        w = [(s, v) for (s, v) in waits if v > 0]
        self.ops[eng].append((fn, w, inc, amt))
        if inc is not None:
            self.cnt[inc] += amt
            return self.cnt[inc]
        return None

    def drain(self, eng):
        self.ops[eng].append((lambda e: e.drain(), [], None, 1))

    def emit(self, block, sems):
        for eng_name in ENGS:
            ops_list = self.ops[eng_name]
            if not ops_list:
                continue
            deco = getattr(block, eng_name)

            def body(eng, ops_list=ops_list):
                hw = {}
                for fn, waits, inc, amt in ops_list:
                    for (s, v) in waits:
                        if v > hw.get(s, 0):
                            eng.wait_ge(sems[s], v)
                            hw[s] = v
                    inst = fn(eng)
                    if inc is not None:
                        inst.then_inc(sems[inc], amt)

            deco(body)


class Banks:
    def __init__(self):
        self.free_gate = [[] for _ in range(8)]
        self.order = list(range(8))

    def acquire(self):
        b = self.order.pop(0)
        return b, self.free_gate[b]

    def release(self, b, waits):
        self.free_gate[b] = list(waits)
        self.order.append(b)


def build_nc(n_seq=4, n_blk=4, taps=None):
    taps = taps or {}
    nc = bass.Bass("TRN2", target_bir_lowering=False)
    NT = n_seq * n_blk

    xh = nc.dram_tensor("x", [n_seq * n_blk * TOK, E], F32, kind="ExternalInput").ap()
    outh = nc.dram_tensor("out", [n_seq * n_blk * TOK, E], F32, kind="ExternalOutput").ap()
    w2h = nc.dram_tensor("w2", [16, 128, 8, 128], F32, kind="ExternalInput").ap()
    wvh = nc.dram_tensor("wv", [128, 8, 1024], F32, kind="ExternalInput").ap()
    wouth = nc.dram_tensor("wout", [128, 8, 1024], F32, kind="ExternalInput").ap()
    wqh = nc.dram_tensor("wq", [128, 8, 1024], I8, kind="ExternalInput").ap()
    wsbh = nc.dram_tensor("wsb", [128, 1024], F32, kind="ExternalInput").ap()
    biasbh = nc.dram_tensor("biasb", [128, 1024], BF16, kind="ExternalInput").ap()
    identh = nc.dram_tensor("ident", [128, 128], BF16, kind="ExternalInput").ap()
    ebiash = nc.dram_tensor("ebias", [128, EB_N], F32, kind="ExternalInput").ap()
    tripoh = nc.dram_tensor("tri_post", [128, 256], BF16, kind="ExternalInput").ap()
    onesh = nc.dram_tensor("ones_bf", [1, 64], BF16, kind="ExternalInput").ap()
    w2bf = nc.dram_tensor("w2bf", [16, 128, 8, 128], BF16, kind="Internal").ap()

    tap_out = {}
    for name, shape, dt in taps.get("list", []):
        tap_out[name] = nc.dram_tensor("tap_" + name, shape, dt, kind="ExternalOutput").ap()

    with ExitStack() as ctx:
        sb = lambda name, shape, dt: ctx.enter_context(nc.sbuf_tensor(name, shape, dt))
        wv_sb = sb("wv_sb", [128, 8 * 1024], BF16)
        wout_sb = sb("wout_sb", [128, 8 * 1024], BF16)
        wq_sb = sb("wq_sb", [128, 8 * 1024], BF16)
        wqk_buf = [sb(f"wqk{i}", [128, 2048], BF16) for i in range(2)]
        wsb_sb = sb("wsb_sb", [128, 1024], F32)
        biasb_sb = sb("biasb_sb", [128, 1024], BF16)
        ident_sb = sb("ident_sb", [128, 128], BF16)
        ebias_sb = sb("ebias_sb", [128, EB_N], F32)
        tripo_sb = sb("tripo_sb", [128, 256], BF16)
        ones_sb = sb("ones_sb", [1, 64], BF16)

        x_tm = [sb(f"x_tm{i}", [128, 4096], F32) for i in range(2)]
        xn4 = sb("xn4", [128, 4096], BF16)
        xq2 = [sb(f"xq2_{i}", [128, 1024], BF16) for i in range(2)]
        xn_fm = sb("xn_fm", [128, 4096], BF16)
        xq_fm = [sb(f"xq_fm{i}", [128, 4096], BF16) for i in range(2)]
        q_fm = sb("q_fm", [128, 4096], BF16)
        k_st = sb("k_st", [128, 8192], BF16)
        v_st = sb("v_st", [128, 8320], BF16)
        o_fm = sb("o_fm", [128, 4096], BF16)
        o_stage = sb("o_stage", [64, 512], BF16)
        pe_buf = sb("pe_buf", [128, 8 * 512], BF16)
        zr_sb = sb("zr_sb", [1, 1024], BF16)
        zln2 = sb("zln2", [1, 1024], F32)
        zb_sb = sb("zb_sb", [64, 1024], BF16)
        ffn1 = [sb(f"ffn1_{i}", [128, 512], F32) for i in range(2)]
        out_hf = [sb(f"outhf{i}", [128, 512], F32) for i in range(2)]
        stt = sb("stt", [128, 64], F32)
        stbn = sb("stbn", [128, 48], F32)
        xs_sb = sb("xs_sb", [128, 8], F32)

        C_MV, C_XMAX, C_XMIN, C_VEPS, C_SQ, C_R0, C_RTMP, C_RSTD = 0, 8, 12, 16, 20, 24, 28, 32
        C_BIASA, C_MX, C_AMAX, C_AM2, C_INV, C_ALPHA, C_BETA = 36, 40, 44, 48, 52, 56, 60


        psum = [ctx.enter_context(nc.psum_tensor(f"ps{i}", [128, 512], F32)) for i in range(8)]
        sem_names = ["X0", "X1", "WA", "WB", "SW", "LG", "SH0", "SH1", "ST0", "ST1", "TP", "T", "V", "S", "G"]
        sems = {s: ctx.enter_context(nc.semaphore(s)) for s in sem_names}
        P = Prog(sem_names)
        banks = Banks()

        def dma(engq, dst, src, inc, amt=16, waits=()):
            return P.op(engq, lambda e, d=dst, s=src: e.dma_start(out=d, in_=s),
                        waits=waits, inc=inc, amt=amt)

        ld0 = 0
        for dst, src in [(wsb_sb[:, :], wsbh), (biasb_sb[:, :], biasbh), (ident_sb[:, :], identh),
                         (ebias_sb[:, :], ebiash), (tripo_sb[:, :], tripoh), (ones_sb[:, :], onesh)]:
            ld0 = dma("sync", dst, src, "SW", waits=[("SW", ld0)])
        lg0 = dma("gpsimd", wv_sb[:, :], wvh.rearrange("p a b -> p (a b)"), "LG")
        lg0 = dma("gpsimd", wout_sb[:, :], wouth.rearrange("p a b -> p (a b)"), "LG",
                  waits=[("LG", lg0)])
        lg0 = dma("gpsimd", wq_sb[:, :], wqh.rearrange("p a b -> p (a b)"), "LG",
                  waits=[("LG", lg0)])
        w2f = w2h.rearrange("a p b c -> a p (b c)")
        w2b = w2bf.rearrange("a p b c -> a p (b c)")
        lgc, stc = P.cnt["LG"], P.cnt["SW"]
        for i in range(2):
            k3 = k_st[:, :].rearrange("p (a b) -> p a b", a=8)
            lgc = dma("gpsimd", k3, w2f[8 * i:8 * (i + 1)].rearrange("a p b -> p a b"),
                      "LG", waits=[("SW", stc), ("LG", lgc)])
            stc = dma("sync", w2b[8 * i:8 * (i + 1)].rearrange("a p b -> p a b"), k3,
                      "SW", waits=[("LG", lgc), ("SW", stc)])
        startup_ST, startup_LD, startup_LG = stc, ld0, lgc

        vones = v_st[:, :].rearrange("p (s h d) -> p s h d", s=8, h=16)[:, :, :, 64:65]
        g_ones = P.op("gpsimd", lambda e: e.memset(vones, 1.0), inc="G")

        x_free = [("X0", 0), ("X1", 0)]
        xn4_free = []
        xq2_free = [[], []]
        wqk_free = [[], []]
        ffn1_free = [[], []]
        outhf_free = [[], []]
        pe_free = [[] for _ in range(8)]
        ostg_free = [[]]
        zr_free = [[], []]
        zb_free = [[], []]
        zln_free = [[], []]
        prevT = 0
        TS = {}

        def tapit(name, src_ap, waits, engq="sync"):
            if name in tap_out:
                dma(engq, tap_out[name], src_ap, "TP",
                    waits=list(waits) + [("TP", P.cnt["TP"])])

        def tgeom(t):
            s_i, n_i = divmod(t, n_blk)
            return s_i, n_i, t % 2, n_i % 2, (n_i - 1) % 2, (s_i * n_blk + n_i) * TOK

        def emit_pre(t):
            s_i, n_i, par, slot, pslot, rowbase = tgeom(t)
            XS = "X%d" % par
            ld_x = dma("sync", x_tm[par][:, :].rearrange("p (a e) -> p a e", a=4),
                       xh[rowbase:rowbase + 512, :].rearrange("(a p) e -> p a e", p=128),
                       XS, waits=[x_free[par]])
            P.drain("vector")
            for tc in range(4):
                xs_ap = x_tm[par][:, 1024 * tc:1024 * (tc + 1)]
                P.op("vector", lambda e, a=xs_ap, tc=tc: e.bn_stats(
                    out=stbn[:, 12 * tc:12 * tc + 6], in_=a[:, 0:512]),
                    waits=[(XS, ld_x)])
                P.op("vector", lambda e, a=xs_ap, tc=tc: e.bn_stats(
                    out=stbn[:, 12 * tc + 6:12 * tc + 12], in_=a[:, 512:1024]))
            P.drain("vector")
            for tc in range(4):
                P.op("vector", lambda e, tc=tc: e.bn_aggr(
                    out=stt[:, C_MV + 2 * tc:C_MV + 2 * tc + 2], in_=stbn[:, 12 * tc:12 * (tc + 1)]))
                P.op("vector", lambda e, tc=tc, a=x_tm[par]: e.reduce_max(
                    out=stt[:, C_XMAX + tc:C_XMAX + tc + 1], in_=a[:, 1024 * tc:1024 * (tc + 1)], axis=AX))
                P.op("vector", lambda e, tc=tc, a=x_tm[par]: e.tensor_reduce(
                    out=stt[:, C_XMIN + tc:C_XMIN + tc + 1], in_=a[:, 1024 * tc:1024 * (tc + 1)],
                    axis=AX, op=ALU.min))
            P.drain("vector")
            mv3 = stt[:, :].rearrange("p (a b) -> p a b", b=2)
            mv_var = mv3[:, 0:4, 1:2]
            mv_mean = mv3[:, 0:4, 0:1]
            P.op("vector", lambda e, mv_var=mv_var: e.tensor_single_scalar(
                out=stt[:, C_VEPS:C_VEPS + 4], in_=mv_var, scalar=1e-5, op=ALU.add))
            P.drain("vector")
            seed_i = stt[:, C_R0:C_R0 + 4].bitcast(mybir.dt.int32)
            P.op("vector", lambda e, seed_i=seed_i: e.tensor_single_scalar(
                out=seed_i, in_=stt[:, C_VEPS:C_VEPS + 4].bitcast(mybir.dt.int32),
                scalar=1, op=ALU.arith_shift_right))
            P.drain("vector")
            P.op("vector", lambda e, seed_i=seed_i: e.tensor_scalar(
                out=seed_i, in0=seed_i, scalar1=-1, scalar2=0x5f3759df,
                op0=ALU.mult, op1=ALU.add))
            P.drain("vector")
            for _nit in range(3):
                last = (_nit == 2)
                P.op("vector", lambda e: e.tensor_tensor(out=stt[:, C_RTMP:C_RTMP + 4],
                                                         in0=stt[:, C_R0:C_R0 + 4],
                                                         in1=stt[:, C_R0:C_R0 + 4], op=ALU.mult))
                P.drain("vector")
                P.op("vector", lambda e: e.tensor_tensor(out=stt[:, C_RTMP:C_RTMP + 4],
                                                         in0=stt[:, C_RTMP:C_RTMP + 4],
                                                         in1=stt[:, C_VEPS:C_VEPS + 4], op=ALU.mult))
                P.drain("vector")
                P.op("vector", lambda e: e.tensor_scalar(out=stt[:, C_RTMP:C_RTMP + 4],
                                                         in0=stt[:, C_RTMP:C_RTMP + 4],
                                                         scalar1=-0.5, scalar2=1.5,
                                                         op0=ALU.mult, op1=ALU.add))
                P.drain("vector")
                dst_col = C_RSTD if last else C_R0
                P.op("vector", lambda e, dst_col=dst_col: e.tensor_tensor(
                    out=stt[:, dst_col:dst_col + 4], in0=stt[:, C_RTMP:C_RTMP + 4],
                    in1=stt[:, C_R0:C_R0 + 4], op=ALU.mult))
                P.drain("vector")
            P.op("vector", lambda e, mv_mean=mv_mean: e.scalar_tensor_tensor(
                out=stt[:, C_BIASA:C_BIASA + 4], in0=mv_mean, scalar=-1.0,
                in1=stt[:, C_RSTD:C_RSTD + 4], op0=ALU.mult, op1=ALU.mult))
            P.op("vector", lambda e, mv_mean=mv_mean: e.tensor_tensor(
                out=stt[:, C_MX:C_MX + 4], in0=stt[:, C_XMAX:C_XMAX + 4],
                in1=mv_mean, op=ALU.subtract))
            P.op("vector", lambda e, mv_mean=mv_mean: e.tensor_tensor(
                out=stt[:, C_RTMP:C_RTMP + 4], in0=mv_mean,
                in1=stt[:, C_XMIN:C_XMIN + 4], op=ALU.subtract))
            P.drain("vector")
            P.op("vector", lambda e: e.tensor_tensor(out=stt[:, C_MX:C_MX + 4],
                                                     in0=stt[:, C_MX:C_MX + 4],
                                                     in1=stt[:, C_RTMP:C_RTMP + 4], op=ALU.max))
            P.drain("vector")
            P.op("vector", lambda e: e.tensor_tensor(out=stt[:, C_AMAX:C_AMAX + 4],
                                                     in0=stt[:, C_MX:C_MX + 4],
                                                     in1=stt[:, C_RSTD:C_RSTD + 4], op=ALU.mult))
            P.drain("vector")
            P.op("vector", lambda e: e.tensor_single_scalar(
                out=stt[:, C_AM2:C_AM2 + 4], in_=stt[:, C_AMAX:C_AMAX + 4], scalar=1e-6, op=ALU.max))
            P.drain("vector")
            P.op("vector", lambda e: e.reciprocal(out=stt[:, C_INV:C_INV + 4], in_=stt[:, C_AM2:C_AM2 + 4]))
            P.op("vector", lambda e, par=par: e.tensor_single_scalar(
                out=xs_sb[:, 4 * par:4 * par + 4], in_=stt[:, C_AM2:C_AM2 + 4],
                scalar=1.0 / 127.0, op=ALU.mult))
            P.drain("vector")
            P.op("vector", lambda e: e.tensor_tensor(out=stt[:, C_RTMP:C_RTMP + 4],
                                                     in0=stt[:, C_INV:C_INV + 4],
                                                     in1=stt[:, C_RSTD:C_RSTD + 4], op=ALU.mult))
            P.drain("vector")
            P.op("vector", lambda e: e.tensor_single_scalar(
                out=stt[:, C_ALPHA:C_ALPHA + 4], in_=stt[:, C_RTMP:C_RTMP + 4], scalar=127.0, op=ALU.mult))
            P.drain("vector")
            P.op("vector", lambda e, mv_mean=mv_mean: e.scalar_tensor_tensor(
                out=stt[:, C_BETA:C_BETA + 4], in0=mv_mean, scalar=-1.0,
                in1=stt[:, C_ALPHA:C_ALPHA + 4], op0=ALU.mult, op1=ALU.mult))
            v_stats = P.op("vector", lambda e: e.engine_nop(), inc="V")
            toks = []
            for tc in range(4):
                xs_ap = x_tm[par][:, 1024 * tc:1024 * (tc + 1)]
                v_xn = P.op("vector", lambda e, tc=tc, a=xs_ap: e.tensor_scalar(
                    out=xn4[:, 1024 * tc:1024 * (tc + 1)], in0=a,
                    scalar1=stt[:, C_RSTD + tc:C_RSTD + tc + 1],
                    scalar2=stt[:, C_BIASA + tc:C_BIASA + tc + 1],
                    op0=ALU.mult, op1=ALU.add),
                    waits=[("V", v_stats)] + xn4_free, inc="V")
                toks.append(("V", v_xn))
            TS[t] = {"ldx": ld_x, "XS": XS, "vstats": v_stats, "xnqtok": toks}
            tapit("stats", stt[:, :], [("V", v_stats)])

        def emit_xnq_transpose(t):
            nonlocal xn4_free
            s_i, n_i, par, slot, pslot, rowbase = tgeom(t)
            v_stats = TS[t]["vstats"]
            toks = TS[t]["xnqtok"]
            vq_toks = {}

            def xq_chain(tc):
                tcp = tc % 2
                xs_ap = x_tm[par][:, 1024 * tc:1024 * (tc + 1)]
                qb, qbw = banks.acquire()
                qb2, qbw2 = banks.acquire()
                P.op("vector", lambda e, tc=tc, a=xs_ap, qb=qb: e.tensor_scalar(
                    out=psum[qb][:, :], in0=a[:, 0:512], scalar1=stt[:, C_ALPHA + tc:C_ALPHA + tc + 1],
                    scalar2=stt[:, C_BETA + tc:C_BETA + tc + 1], op0=ALU.mult, op1=ALU.add),
                    waits=[("V", v_stats)] + qbw, inc="V")
                P.op("vector", lambda e, tc=tc, a=xs_ap, qb2=qb2: e.tensor_scalar(
                    out=psum[qb2][:, :], in0=a[:, 512:1024], scalar1=stt[:, C_ALPHA + tc:C_ALPHA + tc + 1],
                    scalar2=stt[:, C_BETA + tc:C_BETA + tc + 1], op0=ALU.mult, op1=ALU.add),
                    waits=qbw2, inc="V")
                P.drain("vector")
                v_q2 = P.op("vector", lambda e, o=xq2[tcp], qb=qb: e.tensor_scalar(
                    out=o[:, 0:512], in0=psum[qb][:, :], scalar1=MAGIC, scalar2=MAGIC,
                    op0=ALU.add, op1=ALU.subtract), waits=xq2_free[tcp], inc="V")
                banks.release(qb, [("V", v_q2)])
                v_q2 = P.op("vector", lambda e, o=xq2[tcp], qb2=qb2: e.tensor_scalar(
                    out=o[:, 512:1024], in0=psum[qb2][:, :], scalar1=MAGIC, scalar2=MAGIC,
                    op0=ALU.add, op1=ALU.subtract), inc="V")
                banks.release(qb2, [("V", v_q2)])
                vq_toks[tc] = v_q2

            def do_transpose(tc, half):
                tcp = tc % 2
                if half == 0:
                    wgate = [toks[tc]]
                    lsrc, loff, dst_fm = xn4, 1024 * tc, xn_fm
                else:
                    wgate = [("V", vq_toks[tc])]
                    lsrc, loff, dst_fm = xq2[tcp], 0, xq_fm[par]
                for g in range(2):
                    b, bw = banks.acquire()
                    for j in range(4):
                        ec = g * 4 + j
                        tk = P.op("tensor", lambda e, b=b, j=j, ec=ec, lsrc=lsrc, loff=loff: e.matmul(
                            out=psum[b][:, 128 * j:128 * (j + 1)],
                            lhsT=lsrc[:, loff + 128 * ec:loff + 128 * (ec + 1)],
                            rhs=ident_sb[:, :], start=True, stop=True),
                            waits=(wgate + bw + [("SW", startup_LD), ("T", prevT)]) if j == 0 else (),
                            inc="T")
                    dst = dst_fm[:, :].rearrange("p (a b) -> p a b", b=512)[
                        :, g * 4:g * 4 + 4, 128 * tc:128 * (tc + 1)]
                    se = P.op("scalar", lambda e, b=b, d=dst: e.copy(
                        out=d, in_=psum[b][:, :].rearrange("p (a b) -> p a b", a=4)),
                        waits=[("T", tk), ("T", prevT)], inc="S")
                    banks.release(b, [("S", se)])
                if half == 1:
                    xq2_free[tcp] = [("T", P.cnt["T"])]

            xq_chain(0)
            xq_chain(1)
            for tc in range(4):
                do_transpose(tc, 0)
            do_transpose(0, 1)
            xq_chain(2)
            do_transpose(1, 1)
            xq_chain(3)
            do_transpose(2, 1)
            do_transpose(3, 1)
            xn4_free = [("T", P.cnt["T"])]
            TS[t]["sfm"] = ("S", P.cnt["S"])
            tapit("xn_fm", xn_fm[:, :], [TS[t]["sfm"]])
            tapit("xq_fm", xq_fm[par][:, :], [TS[t]["sfm"]])

        def emit_qkv(t):
            s_i, n_i, par, slot, pslot, rowbase = tgeom(t)
            qk_order = list(range(16))
            qk_tok = {}
            fm_gate = TS[t]["sfm"]
            for fgrp in range(8):
                fp = fgrp % 2
                WS = "WA" if fp == 0 else "WB"
                lw = dma("sync", wqk_buf[fp][:, :],
                         w2bf[2 * fgrp:2 * (fgrp + 1)].rearrange("a p b c -> p a (b c)"), WS,
                         waits=wqk_free[fp] + ([("SW", startup_ST)] if (t == 0 and fgrp < 2) else []))
                tk = None
                for fi in range(2):
                    fc = 2 * fgrp + fi
                    b, bw = banks.acquire()
                    for ec in range(8):
                        tk = P.op("tensor", lambda e, b=b, ec=ec, fp=fp, fi=fi: e.matmul(
                            out=psum[b][:, :],
                            lhsT=wqk_buf[fp][:, 1024 * fi + 128 * ec:1024 * fi + 128 * (ec + 1)],
                            rhs=xn_fm[:, 512 * ec:512 * (ec + 1)], start=(ec == 0), stop=(ec == 7)),
                            waits=([(WS, lw), fm_gate] + bw) if ec == 0 else (),
                            inc=("T" if ec == 7 else None))
                    wqk_free[fp] = [("T", tk)]
                    if fc < 8:
                        dst = q_fm[:, 512 * fc:512 * (fc + 1)]
                    else:
                        pr = fc - 8
                        dst = k_st[:, (pr * 2 + slot) * 512:(pr * 2 + slot) * 512 + 512]
                    se = P.op("scalar", lambda e, b=b, d=dst: e.copy(out=d, in_=psum[b][:, :]),
                              waits=[("T", tk), ("T", prevT)] + ([("SW", startup_ST)] if t == 0 else []),
                              inc="S")
                    banks.release(b, [("S", se)])
                    qk_tok[fc % 8] = se
            TS[t]["qw"] = dict(qk_tok)
            v_tok = {}
            for half in range(2):
                for tc in range(4):
                    b, bw = banks.acquire()
                    tk = None
                    for ec in range(8):
                        tk = P.op("tensor", lambda e, b=b, ec=ec, tc=tc, half=half: e.matmul(
                            out=psum[b][:, :],
                            lhsT=xn_fm[:, 512 * ec + 128 * tc:512 * ec + 128 * (tc + 1)],
                            rhs=wv_sb[:, 1024 * ec + 512 * half:1024 * ec + 512 * (half + 1)],
                            start=(ec == 0), stop=(ec == 7)),
                            waits=([fm_gate, ("LG", startup_LG)] + bw) if ec == 0 else (),
                            inc=("T" if ec == 7 else None))
                    dst = v_st[:, :].rearrange("p (s h d) -> p s h d", s=8, h=16)[
                        :, slot * 4 + tc, half * 8:half * 8 + 8, 0:64]
                    se = P.op("scalar", lambda e, b=b, d=dst: e.copy(
                        out=d, in_=psum[b][:, :].rearrange("p (h d) -> p h d", h=8)),
                        waits=[("T", tk), ("G", g_ones), ("T", prevT)], inc="S")
                    banks.release(b, [("S", se)])
                    v_tok[half] = se
            TS[t]["kv"] = dict(v_tok)
            last_tok = P.cnt["S"]
            tapit("q_fm", q_fm[:, :], [("S", last_tok)])
            tapit("k_st", k_st[:, :], [("S", last_tok)])
            tapit("v_st", v_st[:, :], [("S", last_tok)])

        def emit_attn(t, hooks=None):
            nonlocal zr_free, zb_free, zln_free
            hooks = hooks or {}
            s_i, n_i, par, slot, pslot, rowbase = tgeom(t)
            qk_tok = TS[t]["qw"]
            v_tok = TS[t]["kv"]
            chunks = list(range(8)) if n_i > 0 else list(range(4, 8))
            ofm_by_pair = {pr: [] for pr in range(8)}
            divq = []
            divq2 = []

            def k_slice(hh, kw0, w):
                pr, hp = hh // 2, hh % 2
                sl = pslot if kw0 < 512 else slot
                tok0 = kw0 if kw0 < 512 else kw0 - 512
                return k_st[64 * hp:64 * (hp + 1),
                            (pr * 2 + sl) * 512 + tok0:(pr * 2 + sl) * 512 + tok0 + w]

            def v_slice(hh, kw0, ncol):
                sl = pslot if kw0 < 512 else slot
                tok0 = kw0 if kw0 < 512 else kw0 - 512
                tckv = tok0 // 128
                off = ((sl * 4 + tckv) * 16 + hh) * 65
                return v_st[:, off:off + ncol]

            def flush_div_a(k=2):
                nonlocal zr_free, zln_free
                jobs = [divq.pop(0) for _ in range(min(k, len(divq)))]
                for (hh, ob, tkp) in jobs:
                    sl = hh % 2
                    P.op("scalar", lambda e, ob=ob, sl=sl: e.activation(
                        out=zln2[0:1, 512 * sl:512 * (sl + 1)], in_=psum[ob][64:65, :], func=AF.Ln),
                        waits=[("T", tkp)] + zln_free[sl], inc="S")
                    P.drain("scalar")
                    vz = P.op("scalar", lambda e, sl=sl: e.activation(
                        out=zr_sb[0:1, 512 * sl:512 * (sl + 1)],
                        in_=zln2[0:1, 512 * sl:512 * (sl + 1)], func=AF.Exp, scale=-1.0),
                        waits=zr_free[sl], inc="S")
                    zln_free[sl] = [("S", vz)]
                    divq2.append((hh, ob, vz))

            def flush_div_b(k=2):
                nonlocal zr_free, zb_free
                jobs = [divq2.pop(0) for _ in range(min(k, len(divq2)))]
                for (hh, ob, vz) in jobs:
                    sl = hh % 2
                    pr = hh // 2
                    zbk, zbw = banks.acquire()
                    tzb = P.op("tensor", lambda e, zbk=zbk, sl=sl: e.matmul(
                        out=psum[zbk][0:64, :], lhsT=ones_sb[0:1, :],
                        rhs=zr_sb[0:1, 512 * sl:512 * (sl + 1)],
                        start=True, stop=True), waits=[("S", vz)] + zbw, inc="T")
                    zr_free[sl] = [("T", tzb)]
                    szb = P.op("scalar", lambda e, zbk=zbk, sl=sl: e.copy(
                        out=zb_sb[:, 512 * sl:512 * (sl + 1)], in_=psum[zbk][0:64, :]),
                        waits=[("T", tzb)] + zb_free[sl], inc="S")
                    banks.release(zbk, [("S", szb)])
                    if hh % 2 == 0:
                        dsto = o_fm[0:64, pr * 512:pr * 512 + 512]
                        vdo = P.op("vector", lambda e, ob=ob, d=dsto, sl=sl: e.tensor_tensor(
                            out=d, in0=psum[ob][0:64, :], in1=zb_sb[:, 512 * sl:512 * (sl + 1)],
                            op=ALU.mult),
                            waits=[("S", szb), ("T", prevT)], inc="V")
                        banks.release(ob, [("V", vdo)])
                        zb_free[sl] = [("V", vdo)]
                        ofm_by_pair[pr].append(("V", vdo))
                    else:
                        vdo = P.op("vector", lambda e, ob=ob, sl=sl: e.tensor_tensor(
                            out=o_stage[0:64, :], in0=psum[ob][0:64, :],
                            in1=zb_sb[:, 512 * sl:512 * (sl + 1)], op=ALU.mult),
                            waits=[("S", szb)] + ostg_free[0], inc="V")
                        banks.release(ob, [("V", vdo)])
                        zb_free[sl] = [("V", vdo)]
                        sh = dma("gpsimd", o_fm[64:128, pr * 512:pr * 512 + 512],
                                 o_stage[0:64, :], "SH0",
                                 waits=[("V", vdo), ("T", prevT)])
                        ostg_free[0] = [("SH0", sh)]
                        ofm_by_pair[pr].append(("SH0", sh))

            for pr_grp in range(8):
                grp = [2 * pr_grp, 2 * pr_grp + 1]
                obk = {hh: banks.acquire() for hh in grp}
                pv_first = {hh: True for hh in grp}
                tk_last = {}
                pe_slot_i = 0
                pending = []
                kept_last = {hh: [c for c in chunks if _chunk_groups(hh, c, n_i)[0] is not None][-1]
                             for hh in grp}

                def emit_pv(it):
                    (hh, cch, qsp, wp, slp, gmp, sexpp, lastp) = it
                    kw0p = 128 * cch
                    ob, obw = obk[hh]
                    vsl = v_slice(hh, kw0p, 65)
                    if n_i > 0:
                        fresh0, fresh1 = (128 * cch, 128 * (cch + 1)) if cch <= 3 else (None, None)
                    else:
                        fresh0, fresh1 = (0, 512) if cch == 4 else (None, None)
                    segs = []
                    if fresh0 is None:
                        segs.append((qsp, wp))
                    else:
                        assert fresh0 >= qsp
                        if fresh0 > qsp:
                            segs.append((qsp, fresh0 - qsp))
                        segs.append((fresh0, fresh1 - fresh0))
                    tkpv = None
                    for (c0, cw) in segs:
                        tkpv = P.op("tensor", lambda e, ob=ob, c0=c0, cw=cw, qsp=qsp, vsl=vsl,
                                    slp=slp, first=pv_first[hh], lastp=lastp: e.matmul(
                            out=psum[ob][0:65, c0:c0 + cw], lhsT=vsl,
                            rhs=pe_buf[:, 512 * slp + (c0 - qsp):512 * slp + (c0 - qsp) + cw],
                            start=first, stop=lastp),
                            waits=[("V", gmp), ("S", sexpp), ("S", v_tok[hh // 8])]
                                  + (obw if pv_first[hh] else []), inc="T")
                        pv_first[hh] = False
                        obk[hh] = (ob, [])
                    pe_free[slp] = [("T", tkpv)]
                    tk_last[hh] = tkpv

                for ci, c in enumerate(chunks):
                    qs, qe = _trims(c)
                    kw0 = 128 * c
                    cur_items = []
                    for hh in grp:
                        hp2 = hh % 2
                        base2 = 64 * hp2
                        pr2 = hh // 2
                        qsk, gslices = _chunk_groups(hh, c, n_i)
                        if qsk is None:
                            continue
                        w = qe - qsk
                        b, bw = banks.acquire()
                        ksl = k_slice(hh, kw0, 128)
                        tk = P.op("tensor", lambda e, b=b, base2=base2, pr2=pr2,
                                  qsk=qsk, w=w, ksl=ksl: e.matmul(
                            out=psum[b][:, 0:w], lhsT=ksl,
                            rhs=q_fm[base2:base2 + 64, pr2 * 512 + qsk:pr2 * 512 + qsk + w],
                            start=True, stop=True, tile_position=(base2, 0)),
                            waits=bw + [("S", qk_tok[pr2])], inc="T")
                        sl_pe = pe_slot_i % 8
                        pe_slot_i += 1
                        sexp = 0
                        if c <= 3:
                            gslices = gslices[::-1]
                        tri_done = 0
                        ta0 = 128 * c if c <= 3 else qs
                        ta1 = ta0 + 128
                        for gi, (a0, a1, ecol) in enumerate(gslices):
                            sexp = P.op("scalar", lambda e, b=b, a0=a0, a1=a1, qsk=qsk,
                                        sl_pe=sl_pe, ecol=ecol: e.activation(
                                out=pe_buf[:, 512 * sl_pe + a0 - qsk:512 * sl_pe + a1 - qsk],
                                in_=psum[b][:, a0 - qsk:a1 - qsk],
                                func=AF.Exp, scale=SCALE, bias=ebias_sb[:, ecol:ecol + 1]),
                                waits=([("T", tk), ("SW", startup_LD)] + pe_free[sl_pe])
                                      if gi == 0 else (), inc="S")
                            if a0 < ta1 and a1 > ta0:
                                tri_done = sexp
                        banks.release(b, [("S", sexp)])
                        m0 = max(ta0, qsk)
                        if m0 < ta1:
                            toff = m0 - ta0
                            tsel = (128 if c <= 3 else 0) + toff
                            gm = P.op("vector", lambda e, sl_pe=sl_pe, m0=m0, qsk=qsk,
                                      tsel=tsel, mw=ta1 - m0: e.tensor_tensor(
                                out=pe_buf[:, 512 * sl_pe + m0 - qsk:512 * sl_pe + m0 - qsk + mw],
                                in0=pe_buf[:, 512 * sl_pe + m0 - qsk:512 * sl_pe + m0 - qsk + mw],
                                in1=tripo_sb[:, tsel:tsel + mw], op=ALU.mult),
                                waits=[("S", tri_done)], inc="V")
                        else:
                            gm = P.op("vector", lambda e: e.engine_nop(),
                                      waits=[("S", sexp)], inc="V")
                        cur_items.append((hh, c, qsk, w, sl_pe, gm, sexp, c == kept_last[hh]))
                    pending.append(cur_items)
                    if len(pending) > 3:
                        for it in pending.pop(0):
                            emit_pv(it)
                    if ci == 1:
                        flush_div_a(2)
                    if ci == len(chunks) - 2:
                        flush_div_b(2)
                for items in pending:
                    for it in items:
                        emit_pv(it)
                for hh in grp:
                    divq.append((hh, obk[hh][0], tk_last[hh]))
                if pr_grp in hooks:
                    hooks[pr_grp]()
            flush_div_a(len(divq))
            flush_div_b(len(divq2))

            TS[t]["ofm_pair"] = ofm_by_pair
            allofm = [tok for pr in range(8) for tok in ofm_by_pair[pr]]
            tapit("o_fm", o_fm[:, :], allofm)

        def emit_proj(t):
            s_i, n_i, par, slot, pslot, rowbase = tgeom(t)
            ofm_pair = TS[t]["ofm_pair"]
            fm_gate = TS[t]["sfm"]
            for tc in range(4):
                att_b = []
                ffn_b = []
                for half in range(2):
                    b, bw = banks.acquire()
                    tk = None
                    for pr in range(8):
                        tk = P.op("tensor", lambda e, b=b, pr=pr, tc=tc, half=half: e.matmul(
                            out=psum[b][:, :],
                            lhsT=o_fm[:, pr * 512 + 128 * tc:pr * 512 + 128 * (tc + 1)],
                            rhs=wout_sb[:, 1024 * pr + 512 * half:1024 * pr + 512 * (half + 1)],
                            start=(pr == 0), stop=(pr == 7)),
                            waits=ofm_pair[pr] + ((bw + [("LG", startup_LG)]) if pr == 0 else []),
                            inc=("T" if pr == 7 else None))
                    att_b.append((b, tk))
                for half in range(2):
                    b, bw = banks.acquire()
                    tk = None
                    for ec in range(8):
                        tk = P.op("tensor", lambda e, b=b, ec=ec, tc=tc, half=half, par=par: e.matmul(
                            out=psum[b][:, :],
                            lhsT=xq_fm[par][:, 512 * ec + 128 * tc:512 * ec + 128 * (tc + 1)],
                            rhs=wq_sb[:, 1024 * ec + 512 * half:1024 * ec + 512 * (half + 1)],
                            start=(ec == 0), stop=(ec == 7)),
                            waits=(bw + [fm_gate]) if ec == 0 else (),
                            inc=("T" if ec == 7 else None))
                    ffn_b.append((b, tk))
                for half in range(2):
                    ab, atk = att_b[half]
                    fb, ftk = ffn_b[half]
                    u = ffn1[half]
                    v1 = P.op("vector", lambda e, fb=fb, u=u, tc=tc, half=half, par=par: e.scalar_tensor_tensor(
                        out=u[:, :], in0=psum[fb][:, :], scalar=xs_sb[:, 4 * par + tc:4 * par + tc + 1],
                        in1=wsb_sb[:, 512 * half:512 * (half + 1)], op0=ALU.mult, op1=ALU.mult),
                        waits=[("T", ftk)] + ffn1_free[half], inc="V")
                    banks.release(fb, [("V", v1)])
                    P.drain("vector")
                    P.op("vector", lambda e, u=u, half=half: e.tensor_tensor(
                        out=u[:, :], in0=u[:, :], in1=biasb_sb[:, 512 * half:512 * (half + 1)],
                        op=ALU.add))
                    P.drain("vector")
                    P.op("vector", lambda e, u=u, tc=tc, half=half, par=par: e.tensor_tensor(
                        out=u[:, :], in0=u[:, :],
                        in1=x_tm[par][:, 1024 * tc + 512 * half:1024 * tc + 512 * (half + 1)],
                        op=ALU.add))
                    P.drain("vector")
                    v4 = P.op("vector", lambda e, u=u, ab=ab, half=half: e.tensor_tensor(
                        out=out_hf[half][:, :], in0=psum[ab][:, :],
                        in1=u[:, :], op=ALU.add),
                        waits=[("T", atk)] + outhf_free[half], inc="V")
                    banks.release(ab, [("V", v4)])
                    ffn1_free[half] = [("V", v4)]
                    STS = "ST%d" % half
                    st_o = dma("gpsimd",
                               outh[rowbase + 128 * tc:rowbase + 128 * (tc + 1),
                                    512 * half:512 * (half + 1)],
                               out_hf[half][:, :], STS, waits=[("V", v4)])
                    outhf_free[half] = [(STS, st_o)]
            x_free[par] = ("V", P.cnt["V"])

        emit_pre(0)
        emit_xnq_transpose(0)
        emit_qkv(0)
        for t in range(NT):
            if t + 1 < NT:
                emit_pre(t + 1)
            emit_attn(t)
            if t + 1 < NT:
                emit_xnq_transpose(t + 1)
            emit_proj(t)
            prevT = P.cnt["T"]
            if t + 1 < NT:
                emit_qkv(t + 1)
            TS.pop(t - 1, None)

        P.op("sync", lambda e: e.nop(),
             waits=[(s, P.cnt[s]) for s in
                    ("ST0", "ST1", "SH0", "SH1", "X0", "X1", "WA", "WB", "SW", "LG", "TP")])

        block = ctx.enter_context(nc.Block())
        P.emit(block, sems)
    return nc


def _shard_inputs(L0_attn_x, L0_attn_w_in, L0_attn_w_out, L0_ffn_weight_q,
                  L0_ffn_weight_scale, L0_ffn_bias, n_seq=4, n_cores=NCORES):
    x = np.ascontiguousarray(np.asarray(L0_attn_x, np.float32))
    w_in = np.asarray(L0_attn_w_in, np.float32)
    w_out = np.asarray(L0_attn_w_out, np.float32)
    wq = np.asarray(L0_ffn_weight_q, np.int8)
    ws = np.asarray(L0_ffn_weight_scale, np.float32)
    bi = np.asarray(L0_ffn_bias, np.float32)

    w_inT = np.ascontiguousarray(w_in.T)
    w2 = np.ascontiguousarray(w_inT.reshape(8, 128, 24, 128).transpose(2, 1, 0, 3)[0:16])
    wv = np.ascontiguousarray(w_inT[:, 2048:3072].reshape(8, 128, 1024).transpose(1, 0, 2))
    wout = np.ascontiguousarray(w_out.T.reshape(8, 128, 1024).transpose(1, 0, 2))
    wqT = np.ascontiguousarray(wq.T.reshape(8, 128, 1024).transpose(1, 0, 2))
    wsb = np.ascontiguousarray(np.broadcast_to(ws[None, :], (128, 1024)))
    biasb = np.ascontiguousarray(np.broadcast_to(bi[None, :], (128, 1024)).astype(BF))
    consts = _host_constants()

    shared = dict(w2=w2, wv=wv, wout=wout, wq=wqT, wsb=wsb, biasb=biasb,
                  ident=consts["ident"], ebias=consts["ebias"],
                  tri_post=consts["tri_post"], ones_bf=consts["ones_bf"])
    in_maps = []
    for c in range(n_cores):
        m = dict(shared)
        m["x"] = np.ascontiguousarray(x[c * n_seq:(c + 1) * n_seq].reshape(n_seq * SEQ, E))
        in_maps.append(m)
    return in_maps


def kernel(L0_attn_x, L0_attn_w_in, L0_attn_w_out, L0_ffn_weight_q,
           L0_ffn_weight_scale, L0_ffn_bias):
    in_maps = _shard_inputs(L0_attn_x, L0_attn_w_in, L0_attn_w_out,
                            L0_ffn_weight_q, L0_ffn_weight_scale, L0_ffn_bias)
    nc = build_nc(n_seq=4, n_blk=4)
    res = run_bass_kernel_spmd(nc, in_maps, core_ids=list(range(NCORES)))
    out = np.concatenate([r["out"].reshape(4, SEQ, E) for r in res.results], axis=0)
    return np.ascontiguousarray(out.astype(np.float32))


# version 21 (final)
# speedup vs baseline: 1.0657x; 1.0074x over previous
import numpy as np
import ml_dtypes
import concourse.bass as bass
import concourse.mybir as mybir
from concourse.bass_utils import run_bass_kernel_spmd
from contextlib import ExitStack
from concourse import library_config

F32 = mybir.dt.float32
BF16 = mybir.dt.bfloat16
I8 = mybir.dt.int8
ALU = mybir.AluOpType
AF = mybir.ActivationFunctionType
AX = mybir.AxisListType.X
BF = ml_dtypes.bfloat16

H = 16
D = 64
E = 1024
TOK = 512
NCORES = 8
SEQ = 2048
SLOPES = [2.0 ** (-(h + 1) * 8.0 / H) for h in range(H)]
STEEP = [0, 1, 2, 3, 4]
EASY = list(range(5, 16))
NEG = -1.0e9
MAGIC = 12582912.0
SCALE = 0.125

ENGS = ("sync", "tensor", "vector", "scalar", "gpsimd")


def _bf_split(x):
    xf = np.asarray(x, np.float32)
    hi = xf.astype(BF)
    lo = (xf - hi.astype(np.float32)).astype(BF)
    return hi, lo


GROUP_G = [64, 128, 128, 256, 256] + [512] * 11


def _trims(c):
    if c <= 3:
        return 0, 128 * (c + 1)
    return 128 * c - 512, 512


def _ebias_layout():
    idx = {}
    n = 0
    for h in range(16):
        G = GROUP_G[h]
        for c in range(8):
            qs, qe = _trims(c)
            q0 = (qs // G) * G
            for qi0 in range(q0, qe, G):
                idx[(h, c, qi0, False)] = n
                n += 1
        for c in range(4, 8):
            idx[(h, c, 0, True)] = n
            n += 1
    return idx, n


EB_IDX, EB_N = _ebias_layout()


def _chunk_groups(hh, c, n_i):
    qs, qe = _trims(c)
    G = GROUP_G[hh]
    if n_i == 0:
        return qs, [(qs, qe, EB_IDX[(hh, c, 0, True)])]
    q0 = (qs // G) * G
    out = []
    for qi0 in range(q0, qe, G):
        a0, a1 = max(qs, qi0), min(qe, qi0 + G)
        if SLOPES[hh] * (128 * c - qi0 - G / 2.0) > 105.0:
            continue
        out.append((a0, a1, EB_IDX[(hh, c, qi0, False)]))
    if not out:
        return None, []
    qsk = out[0][0]
    return qsk, out


def _host_constants():
    c = {}
    c["ident"] = np.eye(128, dtype=BF)
    p = np.arange(128)
    eb = np.zeros((128, EB_N), np.float32)
    for (h, ch, qi0, blk0), col in EB_IDX.items():
        kw = 128 * ch + p
        if blk0:
            eb[:, col] = SLOPES[h] * (512.0 - kw)
        else:
            eb[:, col] = SLOPES[h] * (qi0 + GROUP_G[h] / 2.0 - kw)
    c["ebias"] = eb
    jj = np.arange(128)[None, :]
    pp = np.arange(128)[:, None]
    tri_post = np.zeros((128, 256), BF)
    tri_post[:, 0:128] = np.where(jj < pp, 0.0, 1.0).astype(BF)
    tri_post[:, 128:256] = np.where(jj > pp, 0.0, 1.0).astype(BF)
    c["tri_post"] = tri_post
    c["ones_bf"] = np.ones((1, 64), BF)
    return c


class Prog:
    def __init__(self, sem_names):
        self.ops = {e: [] for e in ENGS}
        self.cnt = {s: 0 for s in sem_names}

    def op(self, eng, fn, waits=(), inc=None, amt=1):
        w = [(s, v) for (s, v) in waits if v > 0]
        self.ops[eng].append((fn, w, inc, amt))
        if inc is not None:
            self.cnt[inc] += amt
            return self.cnt[inc]
        return None

    def drain(self, eng):
        self.ops[eng].append((lambda e: e.drain(), [], None, 1))

    def emit(self, block, sems):
        for eng_name in ENGS:
            ops_list = self.ops[eng_name]
            if not ops_list:
                continue
            deco = getattr(block, eng_name)

            def body(eng, ops_list=ops_list):
                hw = {}
                for fn, waits, inc, amt in ops_list:
                    for (s, v) in waits:
                        if v > hw.get(s, 0):
                            eng.wait_ge(sems[s], v)
                            hw[s] = v
                    inst = fn(eng)
                    if inc is not None:
                        inst.then_inc(sems[inc], amt)

            deco(body)


class Banks:
    def __init__(self):
        self.free_gate = [[] for _ in range(8)]
        self.order = list(range(8))

    def acquire(self):
        b = self.order.pop(0)
        return b, self.free_gate[b]

    def release(self, b, waits):
        self.free_gate[b] = list(waits)
        self.order.append(b)


def build_nc(n_seq=4, n_blk=4, taps=None):
    taps = taps or {}
    nc = bass.Bass("TRN2", target_bir_lowering=False)
    NT = n_seq * n_blk

    xh = nc.dram_tensor("x", [n_seq * n_blk * TOK, E], F32, kind="ExternalInput").ap()
    outh = nc.dram_tensor("out", [n_seq * n_blk * TOK, E], F32, kind="ExternalOutput").ap()
    w2h = nc.dram_tensor("w2", [16, 128, 8, 128], F32, kind="ExternalInput").ap()
    wvh = nc.dram_tensor("wv", [128, 8, 1024], F32, kind="ExternalInput").ap()
    wouth = nc.dram_tensor("wout", [128, 8, 1024], F32, kind="ExternalInput").ap()
    wqh = nc.dram_tensor("wq", [128, 8, 1024], I8, kind="ExternalInput").ap()
    wsbh = nc.dram_tensor("wsb", [128, 1024], F32, kind="ExternalInput").ap()
    biasbh = nc.dram_tensor("biasb", [128, 1024], BF16, kind="ExternalInput").ap()
    identh = nc.dram_tensor("ident", [128, 128], BF16, kind="ExternalInput").ap()
    ebiash = nc.dram_tensor("ebias", [128, EB_N], F32, kind="ExternalInput").ap()
    tripoh = nc.dram_tensor("tri_post", [128, 256], BF16, kind="ExternalInput").ap()
    onesh = nc.dram_tensor("ones_bf", [1, 64], BF16, kind="ExternalInput").ap()
    w2bf = nc.dram_tensor("w2bf", [16, 128, 8, 128], BF16, kind="Internal").ap()

    tap_out = {}
    for name, shape, dt in taps.get("list", []):
        tap_out[name] = nc.dram_tensor("tap_" + name, shape, dt, kind="ExternalOutput").ap()

    with ExitStack() as ctx:
        sb = lambda name, shape, dt: ctx.enter_context(nc.sbuf_tensor(name, shape, dt))
        wv_sb = sb("wv_sb", [128, 8 * 1024], BF16)
        wout_sb = sb("wout_sb", [128, 8 * 1024], BF16)
        wq_sb = sb("wq_sb", [128, 8 * 1024], BF16)
        wqk_buf = [sb(f"wqk{i}", [128, 2048], BF16) for i in range(2)]
        wsb_sb = sb("wsb_sb", [128, 1024], F32)
        biasb_sb = sb("biasb_sb", [128, 1024], BF16)
        ident_sb = sb("ident_sb", [128, 128], BF16)
        ebias_sb = sb("ebias_sb", [128, EB_N], F32)
        tripo_sb = sb("tripo_sb", [128, 256], BF16)
        ones_sb = sb("ones_sb", [1, 64], BF16)

        x_tm = [sb(f"x_tm{i}", [128, 4096], F32) for i in range(2)]
        xn4 = sb("xn4", [128, 4096], BF16)
        xq2 = [sb(f"xq2_{i}", [128, 1024], BF16) for i in range(2)]
        xn_fm = sb("xn_fm", [128, 4096], BF16)
        xq_fm = [sb(f"xq_fm{i}", [128, 4096], BF16) for i in range(2)]
        q_fm = sb("q_fm", [128, 4096], BF16)
        k_st = sb("k_st", [128, 8192], BF16)
        v_st = sb("v_st", [128, 8320], BF16)
        o_fm = sb("o_fm", [128, 4096], BF16)
        o_stage = sb("o_stage", [64, 512], BF16)
        pe_buf = sb("pe_buf", [128, 8 * 512], BF16)
        zr_sb = sb("zr_sb", [1, 1024], BF16)
        zln2 = sb("zln2", [1, 1024], F32)
        zb_sb = sb("zb_sb", [64, 1024], BF16)
        ffn1 = [sb(f"ffn1_{i}", [128, 512], F32) for i in range(2)]
        out_hf = [sb(f"outhf{i}", [128, 512], F32) for i in range(2)]
        stt = sb("stt", [128, 64], F32)
        stbn = sb("stbn", [128, 48], F32)
        xs_sb = sb("xs_sb", [128, 8], F32)

        C_MV, C_XMAX, C_XMIN, C_VEPS, C_SQ, C_R0, C_RTMP, C_RSTD = 0, 8, 12, 16, 20, 24, 28, 32
        C_BIASA, C_MX, C_AMAX, C_AM2, C_INV, C_ALPHA, C_BETA = 36, 40, 44, 48, 52, 56, 60


        psum = [ctx.enter_context(nc.psum_tensor(f"ps{i}", [128, 512], F32)) for i in range(8)]
        sem_names = ["X0", "X1", "WA", "WB", "SW", "LG", "SH0", "SH1", "ST0", "ST1", "TP", "T", "V", "S", "G"]
        sems = {s: ctx.enter_context(nc.semaphore(s)) for s in sem_names}
        P = Prog(sem_names)
        banks = Banks()

        def dma(engq, dst, src, inc, amt=16, waits=()):
            return P.op(engq, lambda e, d=dst, s=src: e.dma_start(out=d, in_=s),
                        waits=waits, inc=inc, amt=amt)

        ld0 = 0
        for dst, src in [(wsb_sb[:, :], wsbh), (biasb_sb[:, :], biasbh), (ident_sb[:, :], identh),
                         (ebias_sb[:, :], ebiash), (tripo_sb[:, :], tripoh), (ones_sb[:, :], onesh)]:
            ld0 = dma("sync", dst, src, "SW", waits=[("SW", ld0)])
        lg0 = dma("gpsimd", wv_sb[:, :], wvh.rearrange("p a b -> p (a b)"), "LG")
        lg0 = dma("gpsimd", wout_sb[:, :], wouth.rearrange("p a b -> p (a b)"), "LG",
                  waits=[("LG", lg0)])
        lg0 = dma("gpsimd", wq_sb[:, :], wqh.rearrange("p a b -> p (a b)"), "LG",
                  waits=[("LG", lg0)])
        w2f = w2h.rearrange("a p b c -> a p (b c)")
        w2b = w2bf.rearrange("a p b c -> a p (b c)")
        lgc, stc = P.cnt["LG"], P.cnt["SW"]
        for i in range(2):
            k3 = k_st[:, :].rearrange("p (a b) -> p a b", a=8)
            lgc = dma("gpsimd", k3, w2f[8 * i:8 * (i + 1)].rearrange("a p b -> p a b"),
                      "LG", waits=[("SW", stc), ("LG", lgc)])
            stc = dma("sync", w2b[8 * i:8 * (i + 1)].rearrange("a p b -> p a b"), k3,
                      "SW", waits=[("LG", lgc), ("SW", stc)])
        startup_ST, startup_LD, startup_LG = stc, ld0, lgc

        vones = v_st[:, :].rearrange("p (s h d) -> p s h d", s=8, h=16)[:, :, :, 64:65]
        g_ones = P.op("gpsimd", lambda e: e.memset(vones, 1.0), inc="G")

        x_free = [("X0", 0), ("X1", 0)]
        xn4_free = []
        xq2_free = [[], []]
        wqk_free = [[], []]
        ffn1_free = [[], []]
        outhf_free = [[], []]
        pe_free = [[] for _ in range(8)]
        ostg_free = [[]]
        zr_free = [[], []]
        zb_free = [[], []]
        zln_free = [[], []]
        prevT = 0
        TS = {}

        def tapit(name, src_ap, waits, engq="sync"):
            if name in tap_out:
                dma(engq, tap_out[name], src_ap, "TP",
                    waits=list(waits) + [("TP", P.cnt["TP"])])

        def tgeom(t):
            s_i, n_i = divmod(t, n_blk)
            return s_i, n_i, t % 2, n_i % 2, (n_i - 1) % 2, (s_i * n_blk + n_i) * TOK

        def emit_pre(t):
            s_i, n_i, par, slot, pslot, rowbase = tgeom(t)
            XS = "X%d" % par
            ld_x = dma("sync", x_tm[par][:, :].rearrange("p (a e) -> p a e", a=4),
                       xh[rowbase:rowbase + 512, :].rearrange("(a p) e -> p a e", p=128),
                       XS, waits=[x_free[par]])
            P.drain("vector")
            for tc in range(4):
                xs_ap = x_tm[par][:, 1024 * tc:1024 * (tc + 1)]
                P.op("vector", lambda e, a=xs_ap, tc=tc: e.bn_stats(
                    out=stbn[:, 12 * tc:12 * tc + 6], in_=a[:, 0:512]),
                    waits=[(XS, ld_x)])
                P.op("vector", lambda e, a=xs_ap, tc=tc: e.bn_stats(
                    out=stbn[:, 12 * tc + 6:12 * tc + 12], in_=a[:, 512:1024]))
            P.drain("vector")
            for tc in range(4):
                P.op("vector", lambda e, tc=tc: e.bn_aggr(
                    out=stt[:, C_MV + 2 * tc:C_MV + 2 * tc + 2], in_=stbn[:, 12 * tc:12 * (tc + 1)]))
                P.op("vector", lambda e, tc=tc, a=x_tm[par]: e.reduce_max(
                    out=stt[:, C_XMAX + tc:C_XMAX + tc + 1], in_=a[:, 1024 * tc:1024 * (tc + 1)], axis=AX))
                P.op("vector", lambda e, tc=tc, a=x_tm[par]: e.tensor_reduce(
                    out=stt[:, C_XMIN + tc:C_XMIN + tc + 1], in_=a[:, 1024 * tc:1024 * (tc + 1)],
                    axis=AX, op=ALU.min))
            P.drain("vector")
            mv3 = stt[:, :].rearrange("p (a b) -> p a b", b=2)
            mv_var = mv3[:, 0:4, 1:2]
            mv_mean = mv3[:, 0:4, 0:1]
            P.op("vector", lambda e, mv_var=mv_var: e.tensor_single_scalar(
                out=stt[:, C_VEPS:C_VEPS + 4], in_=mv_var, scalar=1e-5, op=ALU.add))
            P.drain("vector")
            seed_i = stt[:, C_R0:C_R0 + 4].bitcast(mybir.dt.int32)
            P.op("vector", lambda e, seed_i=seed_i: e.tensor_single_scalar(
                out=seed_i, in_=stt[:, C_VEPS:C_VEPS + 4].bitcast(mybir.dt.int32),
                scalar=1, op=ALU.arith_shift_right))
            P.drain("vector")
            P.op("vector", lambda e, seed_i=seed_i: e.tensor_scalar(
                out=seed_i, in0=seed_i, scalar1=-1, scalar2=0x5f3759df,
                op0=ALU.mult, op1=ALU.add))
            P.drain("vector")
            for _nit in range(3):
                last = (_nit == 2)
                P.op("vector", lambda e: e.tensor_tensor(out=stt[:, C_RTMP:C_RTMP + 4],
                                                         in0=stt[:, C_R0:C_R0 + 4],
                                                         in1=stt[:, C_R0:C_R0 + 4], op=ALU.mult))
                P.drain("vector")
                P.op("vector", lambda e: e.tensor_tensor(out=stt[:, C_RTMP:C_RTMP + 4],
                                                         in0=stt[:, C_RTMP:C_RTMP + 4],
                                                         in1=stt[:, C_VEPS:C_VEPS + 4], op=ALU.mult))
                P.drain("vector")
                P.op("vector", lambda e: e.tensor_scalar(out=stt[:, C_RTMP:C_RTMP + 4],
                                                         in0=stt[:, C_RTMP:C_RTMP + 4],
                                                         scalar1=-0.5, scalar2=1.5,
                                                         op0=ALU.mult, op1=ALU.add))
                P.drain("vector")
                dst_col = C_RSTD if last else C_R0
                P.op("vector", lambda e, dst_col=dst_col: e.tensor_tensor(
                    out=stt[:, dst_col:dst_col + 4], in0=stt[:, C_RTMP:C_RTMP + 4],
                    in1=stt[:, C_R0:C_R0 + 4], op=ALU.mult))
                P.drain("vector")
            P.op("vector", lambda e, mv_mean=mv_mean: e.scalar_tensor_tensor(
                out=stt[:, C_BIASA:C_BIASA + 4], in0=mv_mean, scalar=-1.0,
                in1=stt[:, C_RSTD:C_RSTD + 4], op0=ALU.mult, op1=ALU.mult))
            P.op("vector", lambda e, mv_mean=mv_mean: e.tensor_tensor(
                out=stt[:, C_MX:C_MX + 4], in0=stt[:, C_XMAX:C_XMAX + 4],
                in1=mv_mean, op=ALU.subtract))
            P.op("vector", lambda e, mv_mean=mv_mean: e.tensor_tensor(
                out=stt[:, C_RTMP:C_RTMP + 4], in0=mv_mean,
                in1=stt[:, C_XMIN:C_XMIN + 4], op=ALU.subtract))
            P.drain("vector")
            P.op("vector", lambda e: e.tensor_tensor(out=stt[:, C_MX:C_MX + 4],
                                                     in0=stt[:, C_MX:C_MX + 4],
                                                     in1=stt[:, C_RTMP:C_RTMP + 4], op=ALU.max))
            P.drain("vector")
            P.op("vector", lambda e: e.tensor_tensor(out=stt[:, C_AMAX:C_AMAX + 4],
                                                     in0=stt[:, C_MX:C_MX + 4],
                                                     in1=stt[:, C_RSTD:C_RSTD + 4], op=ALU.mult))
            P.drain("vector")
            P.op("vector", lambda e: e.tensor_single_scalar(
                out=stt[:, C_AM2:C_AM2 + 4], in_=stt[:, C_AMAX:C_AMAX + 4], scalar=1e-6, op=ALU.max))
            P.drain("vector")
            P.op("vector", lambda e: e.reciprocal(out=stt[:, C_INV:C_INV + 4], in_=stt[:, C_AM2:C_AM2 + 4]))
            P.op("vector", lambda e, par=par: e.tensor_single_scalar(
                out=xs_sb[:, 4 * par:4 * par + 4], in_=stt[:, C_AM2:C_AM2 + 4],
                scalar=1.0 / 127.0, op=ALU.mult))
            P.drain("vector")
            P.op("vector", lambda e: e.tensor_tensor(out=stt[:, C_RTMP:C_RTMP + 4],
                                                     in0=stt[:, C_INV:C_INV + 4],
                                                     in1=stt[:, C_RSTD:C_RSTD + 4], op=ALU.mult))
            P.drain("vector")
            P.op("vector", lambda e: e.tensor_single_scalar(
                out=stt[:, C_ALPHA:C_ALPHA + 4], in_=stt[:, C_RTMP:C_RTMP + 4], scalar=127.0, op=ALU.mult))
            P.drain("vector")
            P.op("vector", lambda e, mv_mean=mv_mean: e.scalar_tensor_tensor(
                out=stt[:, C_BETA:C_BETA + 4], in0=mv_mean, scalar=-1.0,
                in1=stt[:, C_ALPHA:C_ALPHA + 4], op0=ALU.mult, op1=ALU.mult))
            v_stats = P.op("vector", lambda e: e.engine_nop(), inc="V")
            toks = []
            for tc in range(4):
                xs_ap = x_tm[par][:, 1024 * tc:1024 * (tc + 1)]
                v_xn = P.op("vector", lambda e, tc=tc, a=xs_ap: e.tensor_scalar(
                    out=xn4[:, 1024 * tc:1024 * (tc + 1)], in0=a,
                    scalar1=stt[:, C_RSTD + tc:C_RSTD + tc + 1],
                    scalar2=stt[:, C_BIASA + tc:C_BIASA + tc + 1],
                    op0=ALU.mult, op1=ALU.add),
                    waits=[("V", v_stats)] + xn4_free, inc="V")
                toks.append(("V", v_xn))
            TS[t] = {"ldx": ld_x, "XS": XS, "vstats": v_stats, "xnqtok": toks}
            tapit("stats", stt[:, :], [("V", v_stats)])

        def emit_xnq_transpose(t):
            nonlocal xn4_free
            s_i, n_i, par, slot, pslot, rowbase = tgeom(t)
            v_stats = TS[t]["vstats"]
            toks = TS[t]["xnqtok"]
            vq_toks = {}

            def xq_chain(tc):
                tcp = tc % 2
                xs_ap = x_tm[par][:, 1024 * tc:1024 * (tc + 1)]
                qb, qbw = banks.acquire()
                qb2, qbw2 = banks.acquire()
                P.op("vector", lambda e, tc=tc, a=xs_ap, qb=qb: e.tensor_scalar(
                    out=psum[qb][:, :], in0=a[:, 0:512], scalar1=stt[:, C_ALPHA + tc:C_ALPHA + tc + 1],
                    scalar2=stt[:, C_BETA + tc:C_BETA + tc + 1], op0=ALU.mult, op1=ALU.add),
                    waits=[("V", v_stats)] + qbw, inc="V")
                P.op("vector", lambda e, tc=tc, a=xs_ap, qb2=qb2: e.tensor_scalar(
                    out=psum[qb2][:, :], in0=a[:, 512:1024], scalar1=stt[:, C_ALPHA + tc:C_ALPHA + tc + 1],
                    scalar2=stt[:, C_BETA + tc:C_BETA + tc + 1], op0=ALU.mult, op1=ALU.add),
                    waits=qbw2, inc="V")
                P.drain("vector")
                v_q2 = P.op("vector", lambda e, o=xq2[tcp], qb=qb: e.tensor_scalar(
                    out=o[:, 0:512], in0=psum[qb][:, :], scalar1=MAGIC, scalar2=MAGIC,
                    op0=ALU.add, op1=ALU.subtract), waits=xq2_free[tcp], inc="V")
                banks.release(qb, [("V", v_q2)])
                v_q2 = P.op("vector", lambda e, o=xq2[tcp], qb2=qb2: e.tensor_scalar(
                    out=o[:, 512:1024], in0=psum[qb2][:, :], scalar1=MAGIC, scalar2=MAGIC,
                    op0=ALU.add, op1=ALU.subtract), inc="V")
                banks.release(qb2, [("V", v_q2)])
                vq_toks[tc] = v_q2

            def do_transpose(tc, half):
                tcp = tc % 2
                if half == 0:
                    wgate = [toks[tc]]
                    lsrc, loff, dst_fm = xn4, 1024 * tc, xn_fm
                else:
                    wgate = [("V", vq_toks[tc])]
                    lsrc, loff, dst_fm = xq2[tcp], 0, xq_fm[par]
                for g in range(2):
                    b, bw = banks.acquire()
                    for j in range(4):
                        ec = g * 4 + j
                        tk = P.op("tensor", lambda e, b=b, j=j, ec=ec, lsrc=lsrc, loff=loff: e.matmul(
                            out=psum[b][:, 128 * j:128 * (j + 1)],
                            lhsT=lsrc[:, loff + 128 * ec:loff + 128 * (ec + 1)],
                            rhs=ident_sb[:, :], start=True, stop=True),
                            waits=(wgate + bw + [("SW", startup_LD), ("T", prevT)]) if j == 0 else (),
                            inc="T")
                    dst = dst_fm[:, :].rearrange("p (a b) -> p a b", b=512)[
                        :, g * 4:g * 4 + 4, 128 * tc:128 * (tc + 1)]
                    se = P.op("scalar", lambda e, b=b, d=dst: e.copy(
                        out=d, in_=psum[b][:, :].rearrange("p (a b) -> p a b", a=4)),
                        waits=[("T", tk), ("T", prevT)], inc="S")
                    banks.release(b, [("S", se)])
                if half == 1:
                    xq2_free[tcp] = [("T", P.cnt["T"])]

            xq_chain(0)
            xq_chain(1)
            for tc in range(4):
                do_transpose(tc, 0)
            do_transpose(0, 1)
            xq_chain(2)
            do_transpose(1, 1)
            xq_chain(3)
            do_transpose(2, 1)
            do_transpose(3, 1)
            xn4_free = [("T", P.cnt["T"])]
            TS[t]["sfm"] = ("S", P.cnt["S"])
            tapit("xn_fm", xn_fm[:, :], [TS[t]["sfm"]])
            tapit("xq_fm", xq_fm[par][:, :], [TS[t]["sfm"]])

        def emit_qkv(t):
            s_i, n_i, par, slot, pslot, rowbase = tgeom(t)
            qk_order = list(range(16))
            qk_tok = {}
            fm_gate = TS[t]["sfm"]
            for fgrp in range(8):
                fp = fgrp % 2
                WS = "WA" if fp == 0 else "WB"
                lw = dma("sync", wqk_buf[fp][:, :],
                         w2bf[2 * fgrp:2 * (fgrp + 1)].rearrange("a p b c -> p a (b c)"), WS,
                         waits=wqk_free[fp] + ([("SW", startup_ST)] if (t == 0 and fgrp < 2) else []))
                tk = None
                for fi in range(2):
                    fc = 2 * fgrp + fi
                    b, bw = banks.acquire()
                    for ec in range(8):
                        tk = P.op("tensor", lambda e, b=b, ec=ec, fp=fp, fi=fi: e.matmul(
                            out=psum[b][:, :],
                            lhsT=wqk_buf[fp][:, 1024 * fi + 128 * ec:1024 * fi + 128 * (ec + 1)],
                            rhs=xn_fm[:, 512 * ec:512 * (ec + 1)], start=(ec == 0), stop=(ec == 7)),
                            waits=([(WS, lw), fm_gate] + bw) if ec == 0 else (),
                            inc=("T" if ec == 7 else None))
                    wqk_free[fp] = [("T", tk)]
                    if fc < 8:
                        dst = q_fm[:, 512 * fc:512 * (fc + 1)]
                    else:
                        pr = fc - 8
                        dst = k_st[:, (pr * 2 + slot) * 512:(pr * 2 + slot) * 512 + 512]
                    se = P.op("scalar", lambda e, b=b, d=dst: e.copy(out=d, in_=psum[b][:, :]),
                              waits=[("T", tk), ("T", prevT)] + ([("SW", startup_ST)] if t == 0 else []),
                              inc="S")
                    banks.release(b, [("S", se)])
                    qk_tok[fc % 8] = se
            TS[t]["qw"] = dict(qk_tok)
            v_tok = {}
            for half in range(2):
                for tc in range(4):
                    b, bw = banks.acquire()
                    tk = None
                    for ec in range(8):
                        tk = P.op("tensor", lambda e, b=b, ec=ec, tc=tc, half=half: e.matmul(
                            out=psum[b][:, :],
                            lhsT=xn_fm[:, 512 * ec + 128 * tc:512 * ec + 128 * (tc + 1)],
                            rhs=wv_sb[:, 1024 * ec + 512 * half:1024 * ec + 512 * (half + 1)],
                            start=(ec == 0), stop=(ec == 7)),
                            waits=([fm_gate, ("LG", startup_LG)] + bw) if ec == 0 else (),
                            inc=("T" if ec == 7 else None))
                    dst = v_st[:, :].rearrange("p (s h d) -> p s h d", s=8, h=16)[
                        :, slot * 4 + tc, half * 8:half * 8 + 8, 0:64]
                    se = P.op("scalar", lambda e, b=b, d=dst: e.copy(
                        out=d, in_=psum[b][:, :].rearrange("p (h d) -> p h d", h=8)),
                        waits=[("T", tk), ("G", g_ones), ("T", prevT)], inc="S")
                    banks.release(b, [("S", se)])
                    v_tok[half] = se
            TS[t]["kv"] = dict(v_tok)
            last_tok = P.cnt["S"]
            tapit("q_fm", q_fm[:, :], [("S", last_tok)])
            tapit("k_st", k_st[:, :], [("S", last_tok)])
            tapit("v_st", v_st[:, :], [("S", last_tok)])

        def emit_attn(t, hooks=None):
            nonlocal zr_free, zb_free, zln_free
            hooks = hooks or {}
            s_i, n_i, par, slot, pslot, rowbase = tgeom(t)
            qk_tok = TS[t]["qw"]
            v_tok = TS[t]["kv"]
            chunks = list(range(8)) if n_i > 0 else list(range(4, 8))
            ofm_by_pair = {pr: [] for pr in range(8)}
            divq = []
            divq2 = []

            def k_slice(hh, kw0, w):
                pr, hp = hh // 2, hh % 2
                sl = pslot if kw0 < 512 else slot
                tok0 = kw0 if kw0 < 512 else kw0 - 512
                return k_st[64 * hp:64 * (hp + 1),
                            (pr * 2 + sl) * 512 + tok0:(pr * 2 + sl) * 512 + tok0 + w]

            def v_slice(hh, kw0, ncol):
                sl = pslot if kw0 < 512 else slot
                tok0 = kw0 if kw0 < 512 else kw0 - 512
                tckv = tok0 // 128
                off = ((sl * 4 + tckv) * 16 + hh) * 65
                return v_st[:, off:off + ncol]

            def flush_div_a(k=2):
                nonlocal zr_free, zln_free
                jobs = [divq.pop(0) for _ in range(min(k, len(divq)))]
                for (hh, ob, tkp) in jobs:
                    sl = hh % 2
                    P.op("scalar", lambda e, ob=ob, sl=sl: e.activation(
                        out=zln2[0:1, 512 * sl:512 * (sl + 1)], in_=psum[ob][64:65, :], func=AF.Ln),
                        waits=[("T", tkp)] + zln_free[sl], inc="S")
                    P.drain("scalar")
                    vz = P.op("scalar", lambda e, sl=sl: e.activation(
                        out=zr_sb[0:1, 512 * sl:512 * (sl + 1)],
                        in_=zln2[0:1, 512 * sl:512 * (sl + 1)], func=AF.Exp, scale=-1.0),
                        waits=zr_free[sl], inc="S")
                    zln_free[sl] = [("S", vz)]
                    divq2.append((hh, ob, vz))

            def flush_div_b(k=2):
                nonlocal zr_free, zb_free
                jobs = [divq2.pop(0) for _ in range(min(k, len(divq2)))]
                for (hh, ob, vz) in jobs:
                    sl = hh % 2
                    pr = hh // 2
                    zbk, zbw = banks.acquire()
                    tzb = P.op("tensor", lambda e, zbk=zbk, sl=sl: e.matmul(
                        out=psum[zbk][0:64, :], lhsT=ones_sb[0:1, :],
                        rhs=zr_sb[0:1, 512 * sl:512 * (sl + 1)],
                        start=True, stop=True), waits=[("S", vz)] + zbw, inc="T")
                    zr_free[sl] = [("T", tzb)]
                    szb = P.op("scalar", lambda e, zbk=zbk, sl=sl: e.copy(
                        out=zb_sb[:, 512 * sl:512 * (sl + 1)], in_=psum[zbk][0:64, :]),
                        waits=[("T", tzb)] + zb_free[sl], inc="S")
                    banks.release(zbk, [("S", szb)])
                    if hh % 2 == 0:
                        dsto = o_fm[0:64, pr * 512:pr * 512 + 512]
                        vdo = P.op("vector", lambda e, ob=ob, d=dsto, sl=sl: e.tensor_tensor(
                            out=d, in0=psum[ob][0:64, :], in1=zb_sb[:, 512 * sl:512 * (sl + 1)],
                            op=ALU.mult),
                            waits=[("S", szb), ("T", prevT)], inc="V")
                        banks.release(ob, [("V", vdo)])
                        zb_free[sl] = [("V", vdo)]
                        ofm_by_pair[pr].append(("V", vdo))
                    else:
                        vdo = P.op("vector", lambda e, ob=ob, sl=sl: e.tensor_tensor(
                            out=o_stage[0:64, :], in0=psum[ob][0:64, :],
                            in1=zb_sb[:, 512 * sl:512 * (sl + 1)], op=ALU.mult),
                            waits=[("S", szb)] + ostg_free[0], inc="V")
                        banks.release(ob, [("V", vdo)])
                        zb_free[sl] = [("V", vdo)]
                        sh = dma("gpsimd", o_fm[64:128, pr * 512:pr * 512 + 512],
                                 o_stage[0:64, :], "SH0",
                                 waits=[("V", vdo), ("T", prevT)])
                        ostg_free[0] = [("SH0", sh)]
                        ofm_by_pair[pr].append(("SH0", sh))

            for pr_grp in range(8):
                grp = [2 * pr_grp, 2 * pr_grp + 1]
                obk = {hh: banks.acquire() for hh in grp}
                pv_first = {hh: True for hh in grp}
                tk_last = {}
                pe_slot_i = 0
                pending = []
                kept_last = {hh: [c for c in chunks if _chunk_groups(hh, c, n_i)[0] is not None][-1]
                             for hh in grp}

                def emit_pv(it):
                    (hh, cch, qsp, wp, slp, gmp, sexpp, lastp) = it
                    kw0p = 128 * cch
                    ob, obw = obk[hh]
                    vsl = v_slice(hh, kw0p, 65)
                    if n_i > 0:
                        fresh0, fresh1 = (128 * cch, 128 * (cch + 1)) if cch <= 3 else (None, None)
                    else:
                        fresh0, fresh1 = (0, 512) if cch == 4 else (None, None)
                    segs = []
                    if fresh0 is None:
                        segs.append((qsp, wp))
                    else:
                        assert fresh0 >= qsp
                        if fresh0 > qsp:
                            segs.append((qsp, fresh0 - qsp))
                        segs.append((fresh0, fresh1 - fresh0))
                    tkpv = None
                    for (c0, cw) in segs:
                        tkpv = P.op("tensor", lambda e, ob=ob, c0=c0, cw=cw, qsp=qsp, vsl=vsl,
                                    slp=slp, first=pv_first[hh], lastp=lastp: e.matmul(
                            out=psum[ob][0:65, c0:c0 + cw], lhsT=vsl,
                            rhs=pe_buf[:, 512 * slp + (c0 - qsp):512 * slp + (c0 - qsp) + cw],
                            start=first, stop=lastp),
                            waits=[("V", gmp), ("S", sexpp), ("S", v_tok[hh // 8])]
                                  + (obw if pv_first[hh] else []), inc="T")
                        pv_first[hh] = False
                        obk[hh] = (ob, [])
                    pe_free[slp] = [("T", tkpv)]
                    tk_last[hh] = tkpv

                for ci, c in enumerate(chunks):
                    qs, qe = _trims(c)
                    kw0 = 128 * c
                    cur_items = []
                    for hh in grp:
                        hp2 = hh % 2
                        base2 = 64 * hp2
                        pr2 = hh // 2
                        qsk, gslices = _chunk_groups(hh, c, n_i)
                        if qsk is None:
                            continue
                        w = qe - qsk
                        b, bw = banks.acquire()
                        ksl = k_slice(hh, kw0, 128)
                        tk = P.op("tensor", lambda e, b=b, base2=base2, pr2=pr2,
                                  qsk=qsk, w=w, ksl=ksl: e.matmul(
                            out=psum[b][:, 0:w], lhsT=ksl,
                            rhs=q_fm[base2:base2 + 64, pr2 * 512 + qsk:pr2 * 512 + qsk + w],
                            start=True, stop=True, tile_position=(base2, 0)),
                            waits=bw + [("S", qk_tok[pr2])], inc="T")
                        sl_pe = pe_slot_i % 8
                        pe_slot_i += 1
                        sexp = 0
                        if c <= 3:
                            gslices = gslices[::-1]
                        tri_done = 0
                        ta0 = 128 * c if c <= 3 else qs
                        ta1 = ta0 + 128
                        for gi, (a0, a1, ecol) in enumerate(gslices):
                            sexp = P.op("scalar", lambda e, b=b, a0=a0, a1=a1, qsk=qsk,
                                        sl_pe=sl_pe, ecol=ecol: e.activation(
                                out=pe_buf[:, 512 * sl_pe + a0 - qsk:512 * sl_pe + a1 - qsk],
                                in_=psum[b][:, a0 - qsk:a1 - qsk],
                                func=AF.Exp, scale=SCALE, bias=ebias_sb[:, ecol:ecol + 1]),
                                waits=([("T", tk), ("SW", startup_LD)] + pe_free[sl_pe])
                                      if gi == 0 else (), inc="S")
                            if a0 < ta1 and a1 > ta0:
                                tri_done = sexp
                        banks.release(b, [("S", sexp)])
                        m0 = max(ta0, qsk)
                        if m0 < ta1:
                            toff = m0 - ta0
                            tsel = (128 if c <= 3 else 0) + toff
                            gm = P.op("vector", lambda e, sl_pe=sl_pe, m0=m0, qsk=qsk,
                                      tsel=tsel, mw=ta1 - m0: e.tensor_tensor(
                                out=pe_buf[:, 512 * sl_pe + m0 - qsk:512 * sl_pe + m0 - qsk + mw],
                                in0=pe_buf[:, 512 * sl_pe + m0 - qsk:512 * sl_pe + m0 - qsk + mw],
                                in1=tripo_sb[:, tsel:tsel + mw], op=ALU.mult),
                                waits=[("S", tri_done)], inc="V")
                        else:
                            gm = P.op("vector", lambda e: e.engine_nop(),
                                      waits=[("S", sexp)], inc="V")
                        cur_items.append((hh, c, qsk, w, sl_pe, gm, sexp, c == kept_last[hh]))
                    pending.append(cur_items)
                    if len(pending) > 3:
                        for it in pending.pop(0):
                            emit_pv(it)
                    if ci == 0:
                        flush_div_a(2)
                    if ci == 2:
                        flush_div_b(2)
                for items in pending:
                    for it in items:
                        emit_pv(it)
                for hh in grp:
                    divq.append((hh, obk[hh][0], tk_last[hh]))
                if pr_grp in hooks:
                    hooks[pr_grp]()
            flush_div_a(len(divq))
            flush_div_b(len(divq2))

            TS[t]["ofm_pair"] = ofm_by_pair
            allofm = [tok for pr in range(8) for tok in ofm_by_pair[pr]]
            tapit("o_fm", o_fm[:, :], allofm)

        def emit_proj(t):
            s_i, n_i, par, slot, pslot, rowbase = tgeom(t)
            ofm_pair = TS[t]["ofm_pair"]
            fm_gate = TS[t]["sfm"]
            for tc in range(4):
                att_b = []
                ffn_b = []
                for half in range(2):
                    b, bw = banks.acquire()
                    tk = None
                    for pr in range(8):
                        tk = P.op("tensor", lambda e, b=b, pr=pr, tc=tc, half=half: e.matmul(
                            out=psum[b][:, :],
                            lhsT=o_fm[:, pr * 512 + 128 * tc:pr * 512 + 128 * (tc + 1)],
                            rhs=wout_sb[:, 1024 * pr + 512 * half:1024 * pr + 512 * (half + 1)],
                            start=(pr == 0), stop=(pr == 7)),
                            waits=ofm_pair[pr] + ((bw + [("LG", startup_LG)]) if pr == 0 else []),
                            inc=("T" if pr == 7 else None))
                    att_b.append((b, tk))
                for half in range(2):
                    b, bw = banks.acquire()
                    tk = None
                    for ec in range(8):
                        tk = P.op("tensor", lambda e, b=b, ec=ec, tc=tc, half=half, par=par: e.matmul(
                            out=psum[b][:, :],
                            lhsT=xq_fm[par][:, 512 * ec + 128 * tc:512 * ec + 128 * (tc + 1)],
                            rhs=wq_sb[:, 1024 * ec + 512 * half:1024 * ec + 512 * (half + 1)],
                            start=(ec == 0), stop=(ec == 7)),
                            waits=(bw + [fm_gate]) if ec == 0 else (),
                            inc=("T" if ec == 7 else None))
                    ffn_b.append((b, tk))
                for half in range(2):
                    ab, atk = att_b[half]
                    fb, ftk = ffn_b[half]
                    u = ffn1[half]
                    v1 = P.op("vector", lambda e, fb=fb, u=u, tc=tc, half=half, par=par: e.scalar_tensor_tensor(
                        out=u[:, :], in0=psum[fb][:, :], scalar=xs_sb[:, 4 * par + tc:4 * par + tc + 1],
                        in1=wsb_sb[:, 512 * half:512 * (half + 1)], op0=ALU.mult, op1=ALU.mult),
                        waits=[("T", ftk)] + ffn1_free[half], inc="V")
                    banks.release(fb, [("V", v1)])
                    P.drain("vector")
                    P.op("vector", lambda e, u=u, half=half: e.tensor_tensor(
                        out=u[:, :], in0=u[:, :], in1=biasb_sb[:, 512 * half:512 * (half + 1)],
                        op=ALU.add))
                    P.drain("vector")
                    P.op("vector", lambda e, u=u, tc=tc, half=half, par=par: e.tensor_tensor(
                        out=u[:, :], in0=u[:, :],
                        in1=x_tm[par][:, 1024 * tc + 512 * half:1024 * tc + 512 * (half + 1)],
                        op=ALU.add))
                    P.drain("vector")
                    v4 = P.op("vector", lambda e, u=u, ab=ab, half=half: e.tensor_tensor(
                        out=out_hf[half][:, :], in0=psum[ab][:, :],
                        in1=u[:, :], op=ALU.add),
                        waits=[("T", atk)] + outhf_free[half], inc="V")
                    banks.release(ab, [("V", v4)])
                    ffn1_free[half] = [("V", v4)]
                    STS = "ST%d" % half
                    st_o = dma("gpsimd",
                               outh[rowbase + 128 * tc:rowbase + 128 * (tc + 1),
                                    512 * half:512 * (half + 1)],
                               out_hf[half][:, :], STS, waits=[("V", v4)])
                    outhf_free[half] = [(STS, st_o)]
            x_free[par] = ("V", P.cnt["V"])

        emit_pre(0)
        emit_xnq_transpose(0)
        emit_qkv(0)
        for t in range(NT):
            if t + 1 < NT:
                emit_pre(t + 1)
            emit_attn(t)
            if t + 1 < NT:
                emit_xnq_transpose(t + 1)
            emit_proj(t)
            prevT = P.cnt["T"]
            if t + 1 < NT:
                emit_qkv(t + 1)
            TS.pop(t - 1, None)

        P.op("sync", lambda e: e.nop(),
             waits=[(s, P.cnt[s]) for s in
                    ("ST0", "ST1", "SH0", "SH1", "X0", "X1", "WA", "WB", "SW", "LG", "TP")])

        block = ctx.enter_context(nc.Block())
        P.emit(block, sems)
    return nc


def _shard_inputs(L0_attn_x, L0_attn_w_in, L0_attn_w_out, L0_ffn_weight_q,
                  L0_ffn_weight_scale, L0_ffn_bias, n_seq=4, n_cores=NCORES):
    x = np.ascontiguousarray(np.asarray(L0_attn_x, np.float32))
    w_in = np.asarray(L0_attn_w_in, np.float32)
    w_out = np.asarray(L0_attn_w_out, np.float32)
    wq = np.asarray(L0_ffn_weight_q, np.int8)
    ws = np.asarray(L0_ffn_weight_scale, np.float32)
    bi = np.asarray(L0_ffn_bias, np.float32)

    w_inT = np.ascontiguousarray(w_in.T)
    w2 = np.ascontiguousarray(w_inT.reshape(8, 128, 24, 128).transpose(2, 1, 0, 3)[0:16])
    wv = np.ascontiguousarray(w_inT[:, 2048:3072].reshape(8, 128, 1024).transpose(1, 0, 2))
    wout = np.ascontiguousarray(w_out.T.reshape(8, 128, 1024).transpose(1, 0, 2))
    wqT = np.ascontiguousarray(wq.T.reshape(8, 128, 1024).transpose(1, 0, 2))
    wsb = np.ascontiguousarray(np.broadcast_to(ws[None, :], (128, 1024)))
    biasb = np.ascontiguousarray(np.broadcast_to(bi[None, :], (128, 1024)).astype(BF))
    consts = _host_constants()

    shared = dict(w2=w2, wv=wv, wout=wout, wq=wqT, wsb=wsb, biasb=biasb,
                  ident=consts["ident"], ebias=consts["ebias"],
                  tri_post=consts["tri_post"], ones_bf=consts["ones_bf"])
    in_maps = []
    for c in range(n_cores):
        m = dict(shared)
        m["x"] = np.ascontiguousarray(x[c * n_seq:(c + 1) * n_seq].reshape(n_seq * SEQ, E))
        in_maps.append(m)
    return in_maps


def kernel(L0_attn_x, L0_attn_w_in, L0_attn_w_out, L0_ffn_weight_q,
           L0_ffn_weight_scale, L0_ffn_bias):
    in_maps = _shard_inputs(L0_attn_x, L0_attn_w_in, L0_attn_w_out,
                            L0_ffn_weight_q, L0_ffn_weight_scale, L0_ffn_bias)
    nc = build_nc(n_seq=4, n_blk=4)
    res = run_bass_kernel_spmd(nc, in_maps, core_ids=list(range(NCORES)))
    out = np.concatenate([r["out"].reshape(4, SEQ, E) for r in res.results], axis=0)
    return np.ascontiguousarray(out.astype(np.float32))
